# version 12
# baseline (speedup 1.0000x reference)
import math
from contextlib import ExitStack

import numpy as np
import ml_dtypes

import concourse.bass as bass
import concourse.mybir as mybir
from concourse.bass_utils import run_bass_kernel_spmd

F32 = mybir.dt.float32
BF16 = mybir.dt.bfloat16
I8 = mybir.dt.int8
AF = mybir.ActivationFunctionType
ALU = mybir.AluOpType
AX = mybir.AxisListType
BFNP = ml_dtypes.bfloat16

D = 2048
HD = 128
NHC = 4
INTER = 8192
SCALE = 1.0 / math.sqrt(HD)
EPS = 1e-5
MASKVAL = -30000.0

ENGINES = ("sync", "tensor", "vector", "scalar", "gpsimd")


class Emitter:
    def __init__(self):
        self.ops = {e: [] for e in ENGINES}
        self.cnt = {}
        self.lastw = {e: {} for e in ENGINES}

    def op(self, eng, fn, waits=(), inc=None, by=1):
        w = []
        if inc in ("DVE", "ACT", "POOL"):
            waits = list(waits) + [(inc, self.cnt.get(inc, 0))]
        for s, v in waits:
            if v > 0 and v > self.lastw[eng].get(s, 0):
                self.lastw[eng][s] = v
                w.append((s, v))
        t = None
        if inc is not None:
            self.cnt[inc] = self.cnt.get(inc, 0) + by
            t = self.cnt[inc]
        self.ops[eng].append((fn, tuple(w), inc, by))
        return t


def build_program(S):
    assert S % 2048 == 0
    Sq = S // 4
    NC = S // 512
    NT = S // 128
    NQT = Sq // 128
    NTG = Sq // 512
    NKS = S // 128
    NQTILE = S // 512

    nc = bass.Bass("TRN2", target_bir_lowering=False, debug=False)

    def din(name, shape, dt):
        return nc.dram_tensor(name, list(shape), dt, kind="ExternalInput").ap()

    x_full = din("x_full", (S, D), F32)
    x_q = din("x_q", (Sq, D), F32)
    wqkvT = din("wqkvT", (D, 3 * 512), F32)
    bqkvqk_in = din("bqkvqk", (128, 8), F32)
    bqkvv_row = din("bqkvv_row", (1, 512), F32)
    woutT = din("woutT", (512, D), F32)
    bout_row_in = din("bout_row", (1, D), F32)
    acol_in = din("acol", (NHC, 128, S // 128), F32)
    mrows_in = din("mrows", (NHC, 2, S), BF16)
    ones2p_in = din("ones2p", (128, 128), BF16)
    onesfull_in = din("onesfull", (128, 128), BF16)
    masks_in = din("masks", (4, 128, 512), F32)
    ident_in = din("ident", (128, 128), BF16)
    ones2_in = din("ones2", (2, 128), BF16)
    ones1_in = din("ones1", (1, 128), F32)
    wg8 = din("wg8", (D, INTER), I8)
    wu8 = din("wu8", (D, INTER), I8)
    wd8 = din("wd8", (INTER, D), I8)
    sg_in = din("sg_cols", (128, 64), F32)
    su_in = din("su_cols", (128, 64), F32)
    sd_row_in = din("sd_row", (1, D), F32)

    out = nc.dram_tensor("out", [Sq, D], F32, kind="ExternalOutput").ap()

    qfm = nc.dram_tensor("qfm", [512, S], BF16, kind="Internal")
    kfm = nc.dram_tensor("kfm", [512, S], BF16, kind="Internal")
    vtm = nc.dram_tensor("vtm", [S, 512], BF16, kind="Internal")
    wg_bf = nc.dram_tensor("wg_bf", [D, INTER], BF16, kind="Internal")
    wu_bf = nc.dram_tensor("wu_bf", [D, INTER], BF16, kind="Internal")
    wd_bf = nc.dram_tensor("wd_bf", [INTER, D], BF16, kind="Internal")
    rs_in = nc.dram_tensor("rs_in", [S, D], F32, kind="Internal")
    rs_out = nc.dram_tensor("rs_out", [Sq, D], F32, kind="Internal")
    x1d = nc.dram_tensor("x1d", [Sq, D], F32, kind="Internal")

    SBUF_BASE = (nc.sbuf_base + 63) // 64 * 64
    SBUF_CAP = nc.sbuf_top
    cursor = [SBUF_BASE]
    sb_off = {}

    def sb(name, shape, dt, at=None, align=64):
        szb = {F32: 4, BF16: 2, I8: 1}[dt]
        nbytes = shape[1] * szb
        if at is None:
            at = (cursor[0] + align - 1) // align * align
            cursor[0] = at + nbytes
        assert at + nbytes <= SBUF_CAP, f"{name}: {at}+{nbytes} > {SBUF_CAP}"
        sb_off[name] = at
        return nc.alloc_sbuf_tensor_at(name, list(shape), dt, offset=at)

    ident_sb = sb("ident_sb", (128, 128), BF16)
    ones2_sb = sb("ones2_sb", (2, 128), BF16)
    ones1_sb = sb("ones1_sb", (1, 128), F32)
    ones2p_sb = sb("ones2p_sb", (128, 128), BF16)
    onesfull_sb = sb("onesfull_sb", (128, 128), BF16)
    bqkvqk_sb = sb("bqkvqk_sb", (128, 8), F32)
    bqs_sb = sb("bqs_sb", (128, 4), F32)
    sg_sb = sb("sg_sb", (128, 64), F32)
    su_sb = sb("su_sb", (128, 64), F32)
    eps_col = sb("eps_col", (128, 1), F32)
    bv_bc = sb("bv_bc", (128, 512), F32)
    DB = (cursor[0] + 127) // 128 * 128

    cursor[0] = DB
    wqkv_sb = sb("wqkv_sb", (128, 16 * 1536), BF16)
    wstage = [sb(f"wstage{i}", (128, 1536), F32) for i in range(2)]
    x_sb = [sb(f"x_sb{i}", (128, D), F32) for i in range(2)]
    xscr = [sb(f"xscr{i}", (128, D), F32) for i in range(2)]
    xln = [sb(f"xln{i}", (128, D), BF16) for i in range(5)]
    lnfm = [sb(f"lnfm{i}", (128, 16 * 512), BF16) for i in range(2)]
    qkstage = [sb(f"qkstage{i}", (128, 512), BF16) for i in range(4)]
    vstage = [sb(f"vstage{i}", (128, 512), BF16) for i in range(4)]
    stats1 = sb("stats1", (128, 16), F32)
    assert cursor[0] <= SBUF_CAP, f"region A: {cursor[0]}"

    cursor[0] = DB
    ctx_sb = sb("ctx_sb", (128, NHC * S), BF16)
    K_sb = sb("K_sb", (128, S), BF16)
    V_sb = sb("V_sb", (128, S), BF16)
    qt_sb = [sb(f"qt_sb{i}", (128, 512), BF16) for i in range(2)]
    mrows_sb = sb("mrows_sb", (128, S), BF16)
    acol_sb = sb("acol_sb", (128, S // 128), F32)
    P_sb = sb("P_sb", (128, 3 * 512), BF16)
    r_bc = sb("r_bc", (128, 512), F32)
    fl = sb("fl", (128, 48), F32)
    cast_i8 = [sb(f"cast_i8{i}", (128, 2048), I8) for i in range(2)]
    cast_bf = [sb(f"cast_bf{i}", (128, 2048), BF16) for i in range(2)]
    masks_sb = sb("masks_sb", (128, 4 * 512), F32)
    wout_f32 = sb("wout_f32", (128, 2048), F32)
    wout_bf = sb("wout_bf", (128, 4 * 2048), BF16)
    outst3 = [sb(f"outst3_{i}", (128, 512), F32) for i in range(4)]
    assert cursor[0] <= SBUF_CAP, f"region B: {cursor[0]}"

    cursor[0] = DB
    ln2fm = sb("ln2fm", (128, 16 * Sq), BF16)
    fuse_sb = sb("fuse_sb", (128, 64 * 512), BF16)
    sd_bc = sb("sd_bc", (128, D), F32)
    bout_bc = sb("bout_bc", (128, D), F32)
    wgstage = [sb(f"wgstage{i}", (128, 16 * 128), BF16) for i in range(2)]
    wustage = [sb(f"wustage{i}", (128, 16 * 128), BF16) for i in range(2)]
    wd_sb = [sb(f"wd_sb{i}", (128, 2 * 512), BF16) for i in range(2)]
    rs_sb = sb("rs_sb", (128, D), F32)
    scr2 = rs_sb
    xq_sb = sb("xq_sb", (128, D), F32)
    rows_tmp = sb("rows_tmp", (1, D), F32, at=sb_off["xq_sb"])
    x1_sb = sb("x1_sb", (128, D), F32)
    ln2bf = sb("ln2bf", (128, D), BF16)
    stats2 = sb("stats2", (128, 16), F32)
    g_sb = [sb(f"g_sb{i}", (128, 512), BF16) for i in range(2)]
    x1t_sb = [sb(f"x1t{i}", (128, 512), F32) for i in range(2)]
    outst6 = [sb(f"outst6_{i}", (128, 512), F32) for i in range(2)]
    assert cursor[0] <= SBUF_CAP, f"region C: {cursor[0]}"

    ps_f = [nc.alloc_psum_tensor(f"ps_f{i}", [128, 512], F32) for i in range(6)]
    ps_b = [nc.alloc_psum_tensor("ps_b0", [128, 512], BF16),
            nc.alloc_psum_tensor("ps_b1", [128, 512], BF16)]

    E = Emitter()

    def ph0():
        loads = [
            (ident_sb[:, :], ident_in), (ones2_sb[:, :], ones2_in),
            (ones1_sb[:, :], ones1_in), (bqkvqk_sb[:, :], bqkvqk_in),
            (sg_sb[:, :], sg_in), (su_sb[:, :], su_in),
            (ones2p_sb[:, :], ones2p_in), (onesfull_sb[:, :], onesfull_in),
        ]
        for dst, src in loads:
            E.op("sync", lambda e, a=dst, b=src: e.dma_start(out=a, in_=b), inc="Lc", by=16)
        E.op("sync", lambda e: e.dma_start(
            out=wstage[0][0:1, 0:512], in_=bqkvv_row), inc="Lc", by=16)
        lc_total = E.cnt["Lc"]

        E.op("vector", lambda e: e.memset(eps_col[:, :], float(EPS)), inc="DVE")
        E.op("vector", lambda e: e.tensor_scalar_mul(bqs_sb[:, :], bqkvqk_sb[:, 0:4], float(SCALE)),
             waits=[("Lc", lc_total)], inc="DVE")
        pe1 = E.op("tensor", lambda e: e.matmul(ps_f[0][:, :], ones1_sb[:, :],
                                                wstage[0][0:1, 0:512], start=True, stop=True),
                   waits=[("Lc", lc_total)], inc="PE")
        bv_cp = E.op("vector", lambda e: e.tensor_copy(bv_bc[:, :], ps_f[0][:, :]),
                     waits=[("PE", pe1)], inc="DVE")

        pool_done = []
        for dc in range(16):
            slot = dc % 2
            w = [("POOL", pool_done[dc - 2])] if dc >= 2 else []
            if dc == 0:
                w = w + [("PE", pe1)]
            ld = E.op("sync", lambda e, s=slot, d=dc: e.dma_start(
                out=wstage[s][:, :], in_=wqkvT[d * 128:(d + 1) * 128, :]),
                waits=w, inc=f"Lw{slot}", by=16)
            pc = E.op("gpsimd", lambda e, s=slot, d=dc: e.tensor_copy(
                wqkv_sb[:, d * 1536:(d + 1) * 1536], wstage[s][:, :]),
                waits=[(f"Lw{slot}", ld)], inc="POOL")
            pool_done.append(pc)
        return E.cnt["POOL"], bv_cp

    wqkv_pool_cnt, bv_cp = ph0()

    def ph1():
        state = {"pe_last": 0}
        last_apply = {}
        qk_evac_cnt = {}
        v_evac_cnt = {}
        store_cnt = {}
        tp_evac = {}
        lnfm_user = {}
        xsb_user = {}
        xscr_user = {}

        def emit_ln(c):
            for ts in range(4):
                gidx = c * 4 + ts
                slot = gidx % 2
                par = gidx % 2
                ld = E.op("sync", lambda e, s=slot, t0=gidx * 128: e.dma_start(
                    out=x_sb[s][:, :], in_=x_full[t0:t0 + 128, :]),
                    waits=[("DVE", xsb_user.get(slot, 0))], inc=f"Lx{slot}", by=16)
                E.op("vector", lambda e, s=slot, p=par: e.tensor_reduce(
                    stats1[:, 0 + p:1 + p], x_sb[s][:, :], AX.X, ALU.add, negate=True),
                    waits=[(f"Lx{slot}", ld)], inc="DVE")
                nm_dve = E.op("vector", lambda e, p=par: e.tensor_scalar_mul(
                    stats1[:, 2 + p:3 + p], stats1[:, 0 + p:1 + p], 1.0 / D), inc="DVE")
                sq = E.op("scalar", lambda e, s=slot, p=par: e.activation(
                    xscr[s][:, :], x_sb[s][:, :], AF.Square,
                    bias=stats1[:, 2 + p:3 + p], scale=1.0, accum_out=stats1[:, 4 + p:5 + p]),
                    waits=[("DVE", nm_dve), ("ACT", xscr_user.get(slot, 0))], inc="ACT")
                E.op("scalar", lambda e, p=par: e.activation(
                    stats1[:, 6 + p:7 + p], stats1[:, 4 + p:5 + p], AF.Ln,
                    bias=eps_col[:, :], scale=1.0 / D), inc="ACT")
                rstd = E.op("scalar", lambda e, p=par: e.activation(
                    stats1[:, 8 + p:9 + p], stats1[:, 6 + p:7 + p], AF.Exp,
                    bias=0.0, scale=-0.5), inc="ACT")
                xscr_user[slot] = sq
                x5 = gidx % 5
                ap = E.op("vector", lambda e, s=slot, x=x5, p=par: e.tensor_scalar(
                    xln[x][:, :], x_sb[s][:, :], stats1[:, 2 + p:3 + p], stats1[:, 8 + p:9 + p],
                    ALU.add, ALU.mult),
                    waits=[("ACT", rstd), ("PE", state["pe_last"] if gidx >= 4 else 0)],
                    inc="DVE")
                last_apply[gidx] = ap
                xsb_user[slot] = ap

        def emit_qkv(c, ots):
            lbuf = c % 2
            for ot in ots:
                bank = ot % 4
                w = [("DVE", state[("evac", c)]), ("POOL", wqkv_pool_cnt),
                     ("ACT", qk_evac_cnt.get(bank, 0)), ("DVE", v_evac_cnt.get(bank, 0))]
                if ot >= 8:
                    w.append(("DVE", bv_cp))
                for dc in range(16):
                    if ot < 8:
                        lhs = wqkv_sb[:, dc * 1536 + ot * 128: dc * 1536 + (ot + 1) * 128]
                        rhs = lnfm[lbuf][:, dc * 512:(dc + 1) * 512]
                    else:
                        ts = ot - 8
                        lhs = lnfm[lbuf][:, dc * 512 + ts * 128: dc * 512 + (ts + 1) * 128]
                        rhs = wqkv_sb[:, dc * 1536 + 1024: dc * 1536 + 1536]
                    mm = E.op("tensor", lambda e, b=bank, l=lhs, r=rhs, d=dc: e.matmul(
                        ps_f[b][:, :], l, r, start=(d == 0), stop=(d == 15)),
                        waits=w, inc="PE")
                    w = []
                state["pe_last"] = mm
                lnfm_user[lbuf] = mm
                if ot < 4:
                    ev = E.op("scalar", lambda e, b=bank, o=ot: e.activation(
                        qkstage[b][:, :], ps_f[b][:, :], AF.Identity,
                        bias=bqs_sb[:, o:o + 1], scale=float(SCALE)),
                        waits=[("PE", mm), (f"Gq{bank}", store_cnt.get(("qk", bank), 0))],
                        inc="ACT")
                    qk_evac_cnt[bank] = ev
                    store_cnt[("qk", bank)] = E.op("gpsimd", lambda e, b=bank, o=ot, cc=c: e.dma_start(
                        out=qfm.ap()[o * 128:(o + 1) * 128, cc * 512:(cc + 1) * 512],
                        in_=qkstage[b][:, :]), waits=[("ACT", ev)], inc=f"Gq{bank}", by=16)
                elif ot < 8:
                    ev = E.op("scalar", lambda e, b=bank, o=ot: e.activation(
                        qkstage[b][:, :], ps_f[b][:, :], AF.Identity,
                        bias=bqkvqk_sb[:, o:o + 1], scale=1.0),
                        waits=[("PE", mm), (f"Gq{bank}", store_cnt.get(("qk", bank), 0))],
                        inc="ACT")
                    qk_evac_cnt[bank] = ev
                    store_cnt[("qk", bank)] = E.op("gpsimd", lambda e, b=bank, o=ot, cc=c: e.dma_start(
                        out=kfm.ap()[(o - 4) * 128:(o - 3) * 128, cc * 512:(cc + 1) * 512],
                        in_=qkstage[b][:, :]), waits=[("ACT", ev)], inc=f"Gq{bank}", by=16)
                else:
                    ts = ot - 8
                    ev = E.op("vector", lambda e, b=bank, t=ts: e.tensor_tensor(
                        vstage[t][:, :], ps_f[b][:, :], bv_bc[:, :], ALU.add),
                        waits=[("PE", mm), (f"Gv{ts}", store_cnt.get(("v", ts), 0))], inc="DVE")
                    v_evac_cnt[bank] = ev
                    store_cnt[("v", ts)] = E.op("gpsimd", lambda e, t=ts, cc=c: e.dma_start(
                        out=vtm.ap()[cc * 512 + t * 128: cc * 512 + (t + 1) * 128, :],
                        in_=vstage[t][:, :]), waits=[("DVE", ev)], inc=f"Gv{ts}", by=16)

        def emit_transposes(c):
            lbuf = c % 2
            all_evac = 0
            for dc in range(16):
                bk = dc % 2
                w = [("DVE", last_apply[c * 4 + 3]), ("DVE", tp_evac.get(bk, 0))]
                for ts in range(4):
                    x5 = (c * 4 + ts) % 5
                    last = E.op("tensor", lambda e, k=bk, t=ts, x=x5, d=dc: e.transpose(
                        ps_b[k][:, t * 128:(t + 1) * 128],
                        xln[x][:, d * 128:(d + 1) * 128], ident_sb[:, :]),
                        waits=w, inc="PE")
                    w = []
                ev = E.op("vector", lambda e, k=bk, d=dc, b=lbuf: e.tensor_copy(
                    lnfm[b][:, d * 512:(d + 1) * 512], ps_b[k][:, :]),
                    waits=[("PE", last), ("PE", lnfm_user.get(lbuf, 0))], inc="DVE")
                tp_evac[bk] = ev
                all_evac = ev
            state[("evac", c)] = all_evac

        emit_ln(0)
        for c in range(NC):
            emit_transposes(c)
            emit_qkv(c, range(0, 8))
            if c + 1 < NC:
                emit_ln(c + 1)
            emit_qkv(c, range(8, 12))
        waits = [(f"Gq{b}", E.cnt.get(f"Gq{b}", 0)) for b in range(4)]
        waits += [(f"Gv{t}", E.cnt.get(f"Gv{t}", 0)) for t in range(4)]
        return waits

    ph1_store_waits = ph1()

    cast_jobs = [("wout", ic) for ic in range(4)]
    cast_jobs += [("wg", rc, cc) for rc in range(16) for cc in range(4)]
    cast_jobs += [("wu", rc, cc) for rc in range(16) for cc in range(4)]
    cast_jobs += [("wd", rc) for rc in range(64)]

    cast_state = {"i": 0, "pool_prev": {}, "store_prev": {}, "wout_f32_user": 0,
                  "wout_pool": 0}

    def emit_cast_jobs(n):
        for _ in range(n):
            i = cast_state["i"]
            if i >= len(cast_jobs):
                return
            cast_state["i"] = i + 1
            job = cast_jobs[i]
            if job[0] == "wout":
                ic = job[1]
                ld = E.op("sync", lambda e, c=ic: e.dma_start(
                    out=wout_f32[:, :], in_=woutT[c * 128:(c + 1) * 128, :]),
                    waits=[("POOL", cast_state["wout_f32_user"])], inc="Lo", by=16)
                pc = E.op("gpsimd", lambda e, c=ic: e.tensor_copy(
                    wout_bf[:, c * 2048:(c + 1) * 2048], wout_f32[:, :]),
                    waits=[("Lo", ld)], inc="POOL")
                cast_state["wout_f32_user"] = pc
                cast_state["wout_pool"] = pc
                continue
            slot = i % 2
            if job[0] == "wd":
                rc = job[1]
                src = wd8[rc * 128:(rc + 1) * 128, :]
                dst = wd_bf.ap()[rc * 128:(rc + 1) * 128, :]
            else:
                rc, cc = job[1], job[2]
                g8 = wg8 if job[0] == "wg" else wu8
                gb = wg_bf if job[0] == "wg" else wu_bf
                src = g8[rc * 128:(rc + 1) * 128, cc * 2048:(cc + 1) * 2048]
                dst = gb.ap()[rc * 128:(rc + 1) * 128, cc * 2048:(cc + 1) * 2048]
            ld = E.op("sync", lambda e, s=slot, a=src: e.dma_start(
                out=cast_i8[s][:, :], in_=a),
                waits=[("POOL", cast_state["pool_prev"].get(slot, 0))], inc=f"Li{slot}", by=16)
            pc = E.op("gpsimd", lambda e, s=slot: e.tensor_copy(
                cast_bf[s][:, :], cast_i8[s][:, :]),
                waits=[(f"Li{slot}", ld), (f"Gc{slot}", cast_state["store_prev"].get(slot, 0))],
                inc="POOL")
            cast_state["pool_prev"][slot] = pc
            cast_state["store_prev"][slot] = E.op("gpsimd", lambda e, s=slot, d=dst: e.dma_start(
                out=d, in_=cast_bf[s][:, :]), waits=[("POOL", pc)], inc=f"Gc{slot}", by=16)

    def ph2():
        st = {"exp": {}, "qt_user": {}, "ctx_evac": 0, "ctx_free": 0, "l_free": 0,
              "t": 0, "pv_last": 0, "bc_free": 0}
        cast_per_head = (len(cast_jobs) + NHC - 1) // NHC

        def stage_x(info):
            kg, diag_kt, qt = info["kg"], info["diag_kt"], info["qt"]
            bank = st["t"] % 3
            st["t"] += 1
            info["bank"] = bank
            q0 = qt * 512
            w = [("ACT", st["exp"].get(bank, 0)), (f"Lq{info['slot']}", info["qld"]),
                 ("Lk", info["kld"]), ("La", info["ald"]), ("Lm", info["mld"]),
                 ("DVE", st["mz"])]
            E.op("tensor", lambda e, b=bank, k=kg, s=info["slot"]: e.matmul(
                ps_f[b][:, :], K_sb[:, k * 128:(k + 1) * 128], qt_sb[s][:, :],
                start=True, stop=False), waits=w, inc="PE")
            qk = E.op("tensor", lambda e, b=bank, q=q0: e.matmul(
                ps_f[b][:, :], ones2p_sb[:, :], mrows_sb[:, q:q + 512],
                start=False, stop=True), inc="PE")
            mask_c = 0
            if diag_kt is not None:
                mask_c = E.op("vector", lambda e, b=bank, m=diag_kt: e.tensor_tensor(
                    ps_f[b][:, :], ps_f[b][:, :], masks_sb[:, m * 512:(m + 1) * 512],
                    ALU.add), waits=[("PE", qk), ("Lc", st["mkld"])], inc="DVE")
            slot3 = st["t"] % 3
            info["p3"] = slot3
            st["exp"][bank] = E.op("scalar", lambda e, b=bank, k=kg, p3=slot3: e.activation(
                P_sb[:, p3 * 512:(p3 + 1) * 512], ps_f[b][:, :], AF.Exp,
                bias=acol_sb[:, k:k + 1], scale=1.0),
                waits=[("PE", qk), ("DVE", mask_c), ("PE", st["pv_last"])], inc="ACT")
            info["exp_c"] = st["exp"][bank]

        def stage_y(info):
            kg, first, last = info["kg"], info["first"], info["last"]
            w = [("ACT", info["exp_c"]), ("Lv", info["vld"])]
            if first:
                w += [("DVE", st["ctx_free"])]
            pv = E.op("tensor", lambda e, k=kg, p3=info["p3"], st2=first, sp2=last: e.matmul(
                ps_f[3][:, :], V_sb[:, k * 128:(k + 1) * 128],
                P_sb[:, p3 * 512:(p3 + 1) * 512],
                start=st2, stop=sp2), waits=w, inc="PE")
            w = [("DVE", st["l_free"])] if first else []
            st["pv_last"] = E.op("tensor", lambda e, p3=info["p3"], st2=first, sp2=last: e.matmul(
                ps_f[4][:, :], onesfull_sb[:, :], P_sb[:, p3 * 512:(p3 + 1) * 512],
                start=st2, stop=sp2), waits=w, inc="PE")

        def epilogue(info):
            h, qt = info["h"], info["qt"]
            q0 = qt * 512
            pe_pv = E.cnt.get("PE", 0)
            rc = E.op("vector", lambda e: e.reciprocal(r_bc[:, :], ps_f[4][:, :]),
                      waits=[("PE", pe_pv)], inc="DVE")
            st["l_free"] = rc
            st["ctx_evac"] = E.op("vector", lambda e, hh=h, qq=q0: e.tensor_tensor(
                ctx_sb[:, hh * S + qq: hh * S + qq + 512], ps_f[3][:, :], r_bc[:, :],
                ALU.mult), waits=[("DVE", rc)], inc="DVE")
            st["ctx_free"] = st["ctx_evac"]

        pending = None
        mkld = E.op("sync", lambda e: e.dma_start(
            out=masks_sb[:, :].rearrange("p (q k) -> p q k", q=4),
            in_=masks_in.rearrange("q p k -> p q k")),
            waits=ph1_store_waits, inc="Lc", by=16)
        st["mkld"] = mkld
        mz = E.op("vector", lambda e: e.memset(mrows_sb[:, :], 0.0), inc="DVE")
        st["mz"] = mz
        for h in range(NHC):
            if pending is not None:
                stage_y(pending)
                epilogue(pending)
                pending = None
            prev_pe = E.cnt.get("PE", 0)
            kld = E.op("sync", lambda e, hh=h: e.dma_start(
                out=K_sb[:, :], in_=kfm.ap()[hh * 128:(hh + 1) * 128, :]),
                waits=ph1_store_waits + [("PE", prev_pe)], inc="Lk", by=16)
            vld = E.op("sync", lambda e, hh=h: e.dma_start(
                out=V_sb[:, :].rearrange("p (ks d) -> p ks d", ks=NKS),
                in_=vtm.ap()[:, hh * 128:(hh + 1) * 128].rearrange("(ks p) d -> p ks d", p=128)),
                inc="Lv", by=16)
            ald = E.op("sync", lambda e, hh=h: e.dma_start(
                out=acol_sb[:, :], in_=acol_in[hh]), inc="La", by=16)
            mld = E.op("sync", lambda e, hh=h: e.dma_start(
                out=mrows_sb[0:2, :], in_=mrows_in[hh]),
                waits=[("DVE", st["mz"])], inc="Lm", by=16)
            n_cast_emitted = 0
            for qt in range(NQTILE):
                slot = qt % 2
                qld = E.op("sync", lambda e, hh=h, qq=qt * 512, s=slot: e.dma_start(
                    out=qt_sb[s][:, :], in_=qfm.ap()[hh * 128:(hh + 1) * 128, qq:qq + 512]),
                    waits=[("PE", st["qt_user"].get(slot, 0))], inc=f"Lq{slot}", by=16)
                want = ((qt + 1) * cast_per_head) // NQTILE
                emit_cast_jobs(want - n_cast_emitted)
                n_cast_emitted = want
                nkt = (qt + 1) * 4
                for t in range(nkt):
                    c, kt = t // 4, t % 4
                    diag_kt = kt if c == qt else None
                    info = {"kg": t, "qt": qt, "slot": slot, "kld": kld, "ald": ald,
                            "qld": qld, "vld": vld, "mld": mld, "h": h,
                            "diag_kt": diag_kt, "first": (t == 0), "last": (t == nkt - 1)}
                    stage_x(info)
                    if pending is not None:
                        stage_y(pending)
                        if pending["last"]:
                            epilogue(pending)
                    pending = info
                st["qt_user"][slot] = E.cnt.get("PE", 0)
        if pending is not None:
            stage_y(pending)
            epilogue(pending)
        emit_cast_jobs(len(cast_jobs))
        return st["ctx_evac"]

    ctx_done = ph2()

    def ph3():
        bank_user = {}
        st_user = {}
        half_waits = []
        NRS = 4
        for half in range(NRS):
            for gq in range(4):
                for tr in range(NQT // NRS):
                    tt = gq * NQT + half * (NQT // NRS) + tr
                    row0 = half * (S // NRS) + gq * (S // (4 * NRS)) + tr * 128
                    for oc in range(4):
                        bank = (tt * 4 + oc) % 4
                        w = [("DVE", ctx_done), ("POOL", cast_state["wout_pool"]),
                             ("ACT", bank_user.get(bank, 0))]
                        for ic in range(4):
                            mm = E.op("tensor", lambda e, b=bank, i=ic, t=tt, o=oc: e.matmul(
                                ps_f[b][:, :], ctx_sb[:, i * S + t * 128:i * S + t * 128 + 128],
                                wout_bf[:, i * 2048 + o * 512:i * 2048 + (o + 1) * 512],
                                start=(i == 0), stop=(i == 3)), waits=w, inc="PE")
                            w = []
                        slot = (tt * 4 + oc) % 4
                        sem = f"S3{half}_{slot}"
                        prev = st_user.get(slot, ("S30_0", 0))
                        ev = E.op("scalar", lambda e, b=bank, sl=slot: e.activation(
                            outst3[sl][:, :], ps_f[b][:, :], AF.Copy, bias=0.0, scale=1.0),
                            waits=[("PE", mm), prev], inc="ACT")
                        bank_user[bank] = ev
                        stc = E.op("gpsimd", lambda e, sl=slot, r0=row0, o=oc: e.dma_start(
                            out=rs_in.ap()[r0:r0 + 128, o * 512:(o + 1) * 512],
                            in_=outst3[sl][:, :]), waits=[("ACT", ev)], inc=sem, by=16)
                        st_user[slot] = (sem, stc)
            hw = [(f"S3{half}_{sl}", E.cnt.get(f"S3{half}_{sl}", 0)) for sl in range(4)]
            half_waits.append(hw)
        cc = 0
        for half in range(NRS):
            r0, r1 = half * (S // NRS), (half + 1) * (S // NRS)
            o0, o1 = half * (Sq // NRS), (half + 1) * (Sq // NRS)
            cc = E.op("gpsimd", lambda e, a=r0, b=r1, c2=o0, d=o1: e.collective_compute(
                "ReduceScatter", ALU.add,
                replica_groups=[[0, 1, 2, 3], [4, 5, 6, 7]],
                ins=[rs_in.ap()[a:b, :].opt()], outs=[rs_out.ap()[c2:d, :].opt()]),
                waits=half_waits[half], inc="CC")
        return cc

    rs_done = ph3()
    ph3_pe_end = E.cnt.get("PE", 0)

    def ph46():
        ph3_act = E.cnt.get("ACT", 0)
        cast_done = [("Gc0", E.cnt.get("Gc0", 0)), ("Gc1", E.cnt.get("Gc1", 0)),
                     ("POOL", E.cnt.get("POOL", 0))]
        cast_waits = [("Gc0", E.cnt.get("Gc0", 0)), ("Gc1", E.cnt.get("Gc1", 0))]
        ph3_stores = [(f"S3{i}_{j}", E.cnt.get(f"S3{i}_{j}", 0))
                      for i in range(4) for j in range(4)]
        ph3_pe = [("PE", ph3_pe_end)]
        ld = E.op("sync", lambda e: e.dma_start(out=rows_tmp[0:1, :], in_=sd_row_in),
                  waits=cast_done + [("DVE", ctx_done)] + ph3_stores, inc="Lr", by=16)
        bank_cp = {}
        for oc in range(4):
            bank = oc % 2
            pe = E.op("tensor", lambda e, o=oc, b=bank: e.matmul(
                ps_f[b][:, :], ones1_sb[:, :], rows_tmp[0:1, o * 512:(o + 1) * 512],
                start=True, stop=True),
                waits=[("Lr", ld), ("ACT", ph3_act), ("DVE", bank_cp.get(bank, 0))], inc="PE")
            bank_cp[bank] = E.op("vector", lambda e, o=oc, b=bank: e.tensor_copy(
                sd_bc[:, o * 512:(o + 1) * 512], ps_f[b][:, :]), waits=[("PE", pe)], inc="DVE")
        sd_pe = E.cnt.get("PE", 0)
        ld2 = E.op("sync", lambda e: e.dma_start(out=rows_tmp[0:1, :], in_=bout_row_in),
                   waits=[("PE", sd_pe)], inc="Lr", by=16)
        for oc in range(4):
            bank = oc % 2
            pe = E.op("tensor", lambda e, o=oc, b=bank: e.matmul(
                ps_f[b][:, :], ones1_sb[:, :], rows_tmp[0:1, o * 512:(o + 1) * 512],
                start=True, stop=True),
                waits=[("Lr", ld2), ("DVE", bank_cp.get(bank, 0))], inc="PE")
            bank_cp[bank] = E.op("vector", lambda e, o=oc, b=bank: e.tensor_copy(
                bout_bc[:, o * 512:(o + 1) * 512], ps_f[b][:, :]), waits=[("PE", pe)], inc="DVE")
        bcast_done = max(bank_cp.values())
        rows_pe = E.cnt.get("PE", 0)

        p4 = {"a1": 0, "a2": 0, "x1st": 0, "tp": 0, "sq": 0, "tp_ev": {}}
        ln2_tile = {}
        x1st_tile = {}

        def emit_ph4_tile(tt):
            par = tt % 2
            rld = E.op("scalar", lambda e, t=tt: e.dma_start(
                out=rs_sb[:, :], in_=rs_out.ap()[t * 128:(t + 1) * 128, :]),
                waits=[("CC", 1 + tt * 4 // NQT), ("DVE", p4["a1"]),
                       ("ACT", p4["sq"])], inc="Ls", by=16)
            xld = E.op("scalar", lambda e, t=tt: e.dma_start(
                out=xq_sb[:, :], in_=x_q[t * 128:(t + 1) * 128, :]),
                waits=[("DVE", p4["a2"]), ("PE", rows_pe)], inc="Lxq", by=16)
            a1 = E.op("vector", lambda e: e.tensor_tensor(
                x1_sb[:, :], rs_sb[:, :], bout_bc[:, :], ALU.add),
                waits=[("Ls", rld), ("Sx1", p4["x1st"]), ("DVE", bcast_done)], inc="DVE")
            p4["a1"] = a1
            a2 = E.op("vector", lambda e: e.tensor_tensor(
                x1_sb[:, :], x1_sb[:, :], xq_sb[:, :], ALU.add),
                waits=[("Lxq", xld)], inc="DVE")
            p4["a2"] = a2
            p4["x1st"] = E.op("scalar", lambda e, t=tt: e.dma_start(
                out=x1d.ap()[t * 128:(t + 1) * 128, :], in_=x1_sb[:, :]),
                waits=[("DVE", a2)], inc="Sx1", by=16)
            x1st_tile[tt] = p4["x1st"]
            E.op("vector", lambda e, p=par: e.tensor_reduce(
                stats2[:, 0 + p:1 + p], x1_sb[:, :], AX.X, ALU.add, negate=True), inc="DVE")
            nm = E.op("vector", lambda e, p=par: e.tensor_scalar_mul(
                stats2[:, 2 + p:3 + p], stats2[:, 0 + p:1 + p], 1.0 / D), inc="DVE")
            sq = E.op("scalar", lambda e, p=par: e.activation(
                scr2[:, :], x1_sb[:, :], AF.Square,
                bias=stats2[:, 2 + p:3 + p], scale=1.0, accum_out=stats2[:, 4 + p:5 + p]),
                waits=[("DVE", nm)], inc="ACT")
            p4["sq"] = sq
            E.op("scalar", lambda e, p=par: e.activation(
                stats2[:, 6 + p:7 + p], stats2[:, 4 + p:5 + p], AF.Ln,
                bias=eps_col[:, :], scale=1.0 / D), inc="ACT")
            rstd = E.op("scalar", lambda e, p=par: e.activation(
                stats2[:, 8 + p:9 + p], stats2[:, 6 + p:7 + p], AF.Exp,
                bias=0.0, scale=-0.5), inc="ACT")
            ap = E.op("vector", lambda e, p=par: e.tensor_scalar(
                ln2bf[:, :], x1_sb[:, :], stats2[:, 2 + p:3 + p], stats2[:, 8 + p:9 + p],
                ALU.add, ALU.mult), waits=[("ACT", rstd), ("PE", p4["tp"])], inc="DVE")
            for dg in range(4):
                bk = dg % 2
                w = [("DVE", ap), ("DVE", p4["tp_ev"].get(bk, 0))]
                for j in range(4):
                    dc = dg * 4 + j
                    tl = E.op("tensor", lambda e, k2=bk, jj=j, d=dc: e.transpose(
                        ps_b[k2][:, jj * 128:(jj + 1) * 128],
                        ln2bf[:, d * 128:(d + 1) * 128], ident_sb[:, :]),
                        waits=w, inc="PE")
                    w = []
                ev = E.op("vector", lambda e, k2=bk, g=dg, t=tt: e.tensor_copy(
                    ln2fm[:, :].rearrange("p (dc t) -> p dc t", dc=16)[
                        :, g * 4:(g + 1) * 4, t * 128:(t + 1) * 128],
                    ps_b[k2][:, :].rearrange("p (a t) -> p a t", a=4)),
                    waits=[("PE", tl)], inc="DVE")
                p4["tp_ev"][bk] = ev
            ln2_tile[tt] = max(p4["tp_ev"].get(0, 0), p4["tp_ev"].get(1, 0))
            p4["tp"] = E.cnt.get("PE", 0)

        p6 = {"wg_user": {}, "bank_act": {}, "bank_dve": {}, "fuse_pe": 0, "x1_user": {}}

        def emit_tg(tg):
            t0 = tg * 512
            gate_gate = [("DVE", ln2_tile[(tg + 1) * 4 - 1]), ("DVE", bcast_done)]
            for jc in range(64):
                slot = jc % 2
                gb = jc % 2
                ub = 2 + jc % 2
                wld = E.op("sync", lambda e, j=jc, s=slot: e.dma_start(
                    out=wgstage[s][:, :].rearrange("p (a j) -> p a j", a=16),
                    in_=wg_bf.ap()[:, j * 128:(j + 1) * 128].rearrange("(a p) j -> p a j", p=128)),
                    waits=cast_waits + ph3_pe + [("PE", p6["wg_user"].get(("g", slot), 0))],
                    inc=f"Lg{slot}", by=16)
                uld = E.op("sync", lambda e, j=jc, s=slot: e.dma_start(
                    out=wustage[s][:, :].rearrange("p (a j) -> p a j", a=16),
                    in_=wu_bf.ap()[:, j * 128:(j + 1) * 128].rearrange("(a p) j -> p a j", p=128)),
                    waits=[("PE", p6["wg_user"].get(("u", slot), 0))], inc=f"Lu{slot}", by=16)
                w = gate_gate + [(f"Lg{slot}", wld), ("ACT", p6["bank_act"].get(gb, 0))]
                for dc in range(16):
                    mmg = E.op("tensor", lambda e, b=gb, d=dc, s=slot, t=t0: e.matmul(
                        ps_f[b][:, :], wgstage[s][:, d * 128:(d + 1) * 128],
                        ln2fm[:, d * Sq + t:d * Sq + t + 512],
                        start=(d == 0), stop=(d == 15)), waits=w, inc="PE")
                    w = []
                p6["wg_user"][("g", slot)] = mmg
                w = [(f"Lu{slot}", uld), ("DVE", p6["bank_dve"].get(ub, 0))]
                for dc in range(16):
                    mmu = E.op("tensor", lambda e, b=ub, d=dc, s=slot, t=t0: e.matmul(
                        ps_f[b][:, :], wustage[s][:, d * 128:(d + 1) * 128],
                        ln2fm[:, d * Sq + t:d * Sq + t + 512],
                        start=(d == 0), stop=(d == 15)), waits=w, inc="PE")
                    w = []
                p6["wg_user"][("u", slot)] = mmu
                gslot = jc % 2
                sl = E.op("scalar", lambda e, b=gb, g=gslot, j=jc: e.activation(
                    g_sb[g][:, :], ps_f[b][:, :], AF.Silu,
                    bias=0.0, scale=sg_sb[:, j:j + 1]),
                    waits=[("PE", mmg), ("DVE", p6["bank_act"].get(("gs", gslot), 0))],
                    inc="ACT")
                p6["bank_act"][gb] = sl
                fu = E.op("vector", lambda e, b=ub, g=gslot, j=jc: e.scalar_tensor_tensor(
                    fuse_sb[:, j * 512:(j + 1) * 512], ps_f[b][:, :], su_sb[:, j:j + 1],
                    g_sb[g][:, :], ALU.mult, ALU.mult),
                    waits=[("PE", mmu), ("ACT", sl), ("PE", p6["fuse_pe"])], inc="DVE")
                p6["bank_dve"][ub] = fu
                p6["bank_act"][("gs", gslot)] = fu
            fuse_done = E.cnt.get("DVE", 0)
            ev_dve = {}
            st_user = {}
            x1gate = [("Sx1", x1st_tile[(tg + 1) * 4 - 1])]
            for oc in range(4):
                mm = 0
                for jsb in range(32):
                    slot = jsb % 2
                    wld = E.op("sync", lambda e, j=jsb, o=oc, s=slot: e.dma_start(
                        out=wd_sb[s][:, :].rearrange("p (a o) -> p a o", a=2),
                        in_=wd_bf.ap()[j * 256:(j + 1) * 256, o * 512:(o + 1) * 512]
                        .rearrange("(a p) o -> p a o", p=128)),
                        waits=cast_waits + [("PE", p6["wg_user"].get(("wd", slot), 0))],
                        inc=f"Ld{slot}", by=16)
                    for js in range(2):
                        jg = jsb * 2 + js
                        for ts in range(4):
                            w = []
                            if jg == 0:
                                w = [("DVE", fuse_done), (f"Ld{slot}", wld),
                                     ("DVE", ev_dve.get(ts, 0))]
                            elif js == 0 and ts == 0:
                                w = [(f"Ld{slot}", wld)]
                            mm = E.op("tensor", lambda e, j=jg, s=slot, jj=js, t=ts: e.matmul(
                                ps_f[t][:, :],
                                fuse_sb[:, j * 512 + t * 128:j * 512 + (t + 1) * 128],
                                wd_sb[s][:, jj * 512:(jj + 1) * 512],
                                start=(j == 0), stop=(j == 63)), waits=w, inc="PE")
                    p6["wg_user"][("wd", slot)] = mm
                down_pe = mm
                for ts in range(4):
                    slot = ts % 2
                    xld = E.op("sync", lambda e, t=ts, o=oc, g=tg, s=slot: e.dma_start(
                        out=x1t_sb[s][:, :],
                        in_=x1d.ap()[g * 512 + t * 128:g * 512 + (t + 1) * 128,
                                     o * 512:(o + 1) * 512]),
                        waits=x1gate + [("DVE", p6["x1_user"].get(slot, 0))],
                        inc=f"Lt{slot}", by=16)
                    d1 = E.op("vector", lambda e, t=ts, o=oc, s=slot: e.tensor_tensor(
                        outst6[s][:, :], ps_f[t][:, :], sd_bc[:, o * 512:(o + 1) * 512],
                        ALU.mult), waits=[("PE", down_pe), (f"S6{slot}", st_user.get(slot, 0))],
                        inc="DVE")
                    d2 = E.op("vector", lambda e, s=slot, t=ts: e.tensor_tensor(
                        outst6[s][:, :], outst6[s][:, :], x1t_sb[s][:, :], ALU.add),
                        waits=[(f"Lt{slot}", xld)], inc="DVE")
                    ev_dve[ts] = d2
                    p6["x1_user"][slot] = d2
                    st_user[slot] = E.op("scalar", lambda e, t=ts, o=oc, g=tg, s=slot: e.dma_start(
                        out=out[g * 512 + t * 128:g * 512 + (t + 1) * 128, o * 512:(o + 1) * 512],
                        in_=outst6[s][:, :]), waits=[("DVE", d2)], inc=f"S6{slot}", by=16)
            p6["fuse_pe"] = down_pe

        for tg in range(NTG):
            for tt in range(tg * 4, (tg + 1) * 4):
                emit_ph4_tile(tt)
            emit_tg(tg)

    ph46()

    final_keys = ["Sx1", "S60", "S61"] + [f"S3{i}_{j}" for i in range(4) for j in range(4)]
    final_waits = [(k, E.cnt[k]) for k in final_keys if k in E.cnt]

    sem_keys = set(E.cnt.keys())
    for e in ENGINES:
        for _, ws, _, _ in E.ops[e]:
            for s, _v in ws:
                sem_keys.add(s)
    sem_keys = sorted(sem_keys)

    with ExitStack() as stk:
        sems = {s: stk.enter_context(nc.semaphore(f"sem_{s}")) for s in sem_keys}
        block = stk.enter_context(nc.Block())

        def run_engine(handle, eng):
            for fn, waits, inc, by in E.ops[eng]:
                for s, v in waits:
                    handle.wait_ge(sems[s], v)
                ins = fn(handle)
                if inc is not None:
                    ins.then_inc(sems[inc], by)
            if eng in ("scalar", "gpsimd"):
                for s, v in final_waits:
                    handle.wait_ge(sems[s], v)

        block.sync(lambda e: run_engine(e, "sync"))
        block.tensor(lambda e: run_engine(e, "tensor"))
        block.vector(lambda e: run_engine(e, "vector"))
        block.scalar(lambda e: run_engine(e, "scalar"))
        block.gpsimd(lambda e: run_engine(e, "gpsimd"))

    return nc



MBOUND_C = 45.0


def _acol(S, g):
    pos = np.arange(1 - S, 1, dtype=np.float32)
    outm = np.zeros((NHC, 128, S // 128), dtype=np.float32)
    for j in range(NHC):
        hglob = 4 * g + j
        slope = np.float32(1.0 / 2.0 ** np.float32((hglob + 1) * 0.5))
        outm[j] = (pos * slope).reshape(S // 128, 128).T
    return outm


def _mrows(S, g):
    pos = np.arange(1 - S, 1, dtype=np.float32)
    outa = np.zeros((NHC, 2, S), dtype=BFNP)
    for j in range(NHC):
        hglob = 4 * g + j
        slope = np.float32(1.0 / 2.0 ** np.float32((hglob + 1) * 0.5))
        m = -(pos * slope + np.float32(MBOUND_C))
        hi = m.astype(BFNP)
        lo = (m - hi.astype(np.float32)).astype(BFNP)
        outa[j, 0] = hi
        outa[j, 1] = lo
    return outa


def _masks():
    m = np.empty((4, 128, 512), dtype=np.float32)
    i = np.arange(128)[:, None]
    j = np.arange(512)[None, :]
    for kt in range(4):
        m[kt] = np.where(i <= j - kt * 128, 0.0, MASKVAL)
    return m


def kernel(L0_attn_x, L0_attn_Wqkv, L0_attn_b_qkv, L0_attn_Wout, L0_attn_b_out,
           L0_ffn_w_gate, L0_ffn_w_up, L0_ffn_w_down,
           L0_ffn_s_gate, L0_ffn_s_up, L0_ffn_s_down):
    x = np.ascontiguousarray(np.asarray(L0_attn_x, dtype=np.float32))
    B, S, _ = x.shape
    Sq = S // 4
    Wqkv = np.asarray(L0_attn_Wqkv, dtype=np.float32)
    b_qkv = np.asarray(L0_attn_b_qkv, dtype=np.float32)
    Wout = np.asarray(L0_attn_Wout, dtype=np.float32)
    b_out = np.asarray(L0_attn_b_out, dtype=np.float32)
    wg = np.ascontiguousarray(np.asarray(L0_ffn_w_gate))
    wu = np.ascontiguousarray(np.asarray(L0_ffn_w_up))
    wd = np.ascontiguousarray(np.asarray(L0_ffn_w_down))
    sg = np.asarray(L0_ffn_s_gate, dtype=np.float32)
    su = np.asarray(L0_ffn_s_up, dtype=np.float32)
    sd = np.asarray(L0_ffn_s_down, dtype=np.float32)

    nc = build_program(S)

    masks = _masks()
    ident = np.eye(128, dtype=BFNP)
    ones2 = np.ones((2, 128), dtype=BFNP)
    ones1 = np.ones((1, 128), dtype=np.float32)
    sg_cols = np.ascontiguousarray(sg.reshape(64, 128).T)
    su_cols = np.ascontiguousarray(su.reshape(64, 128).T)
    sd_row = np.ascontiguousarray(sd.reshape(1, D))
    bout_row = np.ascontiguousarray(b_out.reshape(1, D))

    in_maps = []
    for c in range(8):
        b, g = c // 4, c % 4
        q_rows = Wqkv[512 * g:512 * (g + 1)]
        k_rows = Wqkv[D + 512 * g:D + 512 * (g + 1)]
        v_rows = Wqkv[2 * D + 512 * g:2 * D + 512 * (g + 1)]
        wqkvT_s = np.ascontiguousarray(np.concatenate([q_rows, k_rows, v_rows], axis=0).T)
        bq = b_qkv[512 * g:512 * (g + 1)]
        bk = b_qkv[D + 512 * g:D + 512 * (g + 1)]
        bqkvqk = np.ascontiguousarray(
            np.concatenate([bq.reshape(4, 128), bk.reshape(4, 128)], axis=0).T)
        bqkvv_row = np.ascontiguousarray(
            b_qkv[2 * D + 512 * g:2 * D + 512 * (g + 1)].reshape(1, 512))
        woutT_s = np.ascontiguousarray(Wout[:, 512 * g:512 * (g + 1)].T)
        in_maps.append(dict(
            x_full=x[b], x_q=np.ascontiguousarray(x[b, g * Sq:(g + 1) * Sq]),
            wqkvT=wqkvT_s, bqkvqk=bqkvqk, bqkvv_row=bqkvv_row,
            woutT=woutT_s, bout_row=bout_row,
            acol=_acol(S, g), mrows=_mrows(S, g), masks=masks, ident=ident,
            ones2=ones2, ones1=ones1,
            ones2p=np.concatenate([np.ones((2, 128), dtype=BFNP),
                                   np.zeros((126, 128), dtype=BFNP)]),
            onesfull=np.ones((128, 128), dtype=BFNP),
            wg8=wg, wu8=wu, wd8=wd,
            sg_cols=sg_cols, su_cols=su_cols, sd_row=sd_row,
        ))

    res = run_bass_kernel_spmd(nc, in_maps, core_ids=list(range(8)))

    y = np.empty((B, S, D), dtype=np.float32)
    for c in range(8):
        b, g = c // 4, c % 4
        y[b, g * Sq:(g + 1) * Sq] = res.results[c]["out"]
    return y


# version 13
# speedup vs baseline: 1.0477x; 1.0477x over previous
import math
from contextlib import ExitStack

import numpy as np
import ml_dtypes

import concourse.bass as bass
import concourse.mybir as mybir
from concourse.bass_utils import run_bass_kernel_spmd

F32 = mybir.dt.float32
BF16 = mybir.dt.bfloat16
I8 = mybir.dt.int8
AF = mybir.ActivationFunctionType
ALU = mybir.AluOpType
AX = mybir.AxisListType
BFNP = ml_dtypes.bfloat16

D = 2048
HD = 128
NHC = 4
INTER = 8192
SCALE = 1.0 / math.sqrt(HD)
EPS = 1e-5
MASKVAL = -30000.0

ENGINES = ("sync", "tensor", "vector", "scalar", "gpsimd")


class Emitter:
    def __init__(self):
        self.ops = {e: [] for e in ENGINES}
        self.cnt = {}
        self.lastw = {e: {} for e in ENGINES}

    def op(self, eng, fn, waits=(), inc=None, by=1):
        w = []
        if inc in ("DVE", "ACT", "POOL"):
            waits = list(waits) + [(inc, self.cnt.get(inc, 0))]
        for s, v in waits:
            if v > 0 and v > self.lastw[eng].get(s, 0):
                self.lastw[eng][s] = v
                w.append((s, v))
        t = None
        if inc is not None:
            self.cnt[inc] = self.cnt.get(inc, 0) + by
            t = self.cnt[inc]
        self.ops[eng].append((fn, tuple(w), inc, by))
        return t


def build_program(S):
    assert S % 2048 == 0
    Sq = S // 4
    NC = S // 512
    NT = S // 128
    NQT = Sq // 128
    NTG = Sq // 512
    NKS = S // 128
    NQTILE = S // 512

    nc = bass.Bass("TRN2", target_bir_lowering=False, debug=False)

    def din(name, shape, dt):
        return nc.dram_tensor(name, list(shape), dt, kind="ExternalInput").ap()

    x_full = din("x_full", (S, D), F32)
    x_q = din("x_q", (Sq, D), F32)
    wqkvT = din("wqkvT", (D, 3 * 512), F32)
    bqkvqk_in = din("bqkvqk", (128, 8), F32)
    bqkvv_row = din("bqkvv_row", (1, 512), F32)
    woutT = din("woutT", (512, D), F32)
    bout_row_in = din("bout_row", (1, D), F32)
    acol_in = din("acol", (NHC, 128, S // 128), F32)
    mrows_in = din("mrows", (NHC, 2, S), BF16)
    ones2p_in = din("ones2p", (128, 128), BF16)
    onesfull_in = din("onesfull", (128, 128), BF16)
    masks_in = din("masks", (4, 128, 512), F32)
    ident_in = din("ident", (128, 128), BF16)
    ones2_in = din("ones2", (2, 128), BF16)
    ones1_in = din("ones1", (1, 128), F32)
    wg8 = din("wg8", (D, INTER), I8)
    wu8 = din("wu8", (D, INTER), I8)
    wd8 = din("wd8", (INTER, D), I8)
    sg_in = din("sg_cols", (128, 64), F32)
    su_in = din("su_cols", (128, 64), F32)
    sd_row_in = din("sd_row", (1, D), F32)

    out = nc.dram_tensor("out", [Sq, D], F32, kind="ExternalOutput").ap()

    qfm = nc.dram_tensor("qfm", [512, S], BF16, kind="Internal")
    kfm = nc.dram_tensor("kfm", [512, S], BF16, kind="Internal")
    vtm = nc.dram_tensor("vtm", [S, 512], BF16, kind="Internal")
    wg_bf = nc.dram_tensor("wg_bf", [D, INTER], BF16, kind="Internal")
    wu_bf = nc.dram_tensor("wu_bf", [D, INTER], BF16, kind="Internal")
    wd_bf = nc.dram_tensor("wd_bf", [INTER, D], BF16, kind="Internal")
    rs_in = nc.dram_tensor("rs_in", [S, D], F32, kind="Internal")
    rs_out = nc.dram_tensor("rs_out", [Sq, D], F32, kind="Internal")
    x1d = nc.dram_tensor("x1d", [Sq, D], F32, kind="Internal")

    SBUF_BASE = (nc.sbuf_base + 63) // 64 * 64
    SBUF_CAP = nc.sbuf_top
    cursor = [SBUF_BASE]
    sb_off = {}

    def sb(name, shape, dt, at=None, align=64):
        szb = {F32: 4, BF16: 2, I8: 1}[dt]
        nbytes = shape[1] * szb
        if at is None:
            at = (cursor[0] + align - 1) // align * align
            cursor[0] = at + nbytes
        assert at + nbytes <= SBUF_CAP, f"{name}: {at}+{nbytes} > {SBUF_CAP}"
        sb_off[name] = at
        return nc.alloc_sbuf_tensor_at(name, list(shape), dt, offset=at)

    ident_sb = sb("ident_sb", (128, 128), BF16)
    ones2_sb = sb("ones2_sb", (2, 128), BF16)
    ones1_sb = sb("ones1_sb", (1, 128), F32)
    ones2p_sb = sb("ones2p_sb", (128, 128), BF16)
    onesfull_sb = sb("onesfull_sb", (128, 128), BF16)
    bqkvqk_sb = sb("bqkvqk_sb", (128, 8), F32)
    bqs_sb = sb("bqs_sb", (128, 4), F32)
    sg_sb = sb("sg_sb", (128, 64), F32)
    su_sb = sb("su_sb", (128, 64), F32)
    eps_col = sb("eps_col", (128, 1), F32)
    bv_bc = sb("bv_bc", (128, 512), F32)
    DB = (cursor[0] + 127) // 128 * 128

    cursor[0] = DB
    wqkv_sb = sb("wqkv_sb", (128, 16 * 1536), BF16)
    wstage = [sb(f"wstage{i}", (128, 1536), F32) for i in range(2)]
    x_sb = [sb(f"x_sb{i}", (128, D), F32) for i in range(2)]
    xscr = [sb(f"xscr{i}", (128, D), F32) for i in range(2)]
    xln = [sb(f"xln{i}", (128, D), BF16) for i in range(5)]
    lnfm = [sb(f"lnfm{i}", (128, 16 * 512), BF16) for i in range(2)]
    qkstage = [sb(f"qkstage{i}", (128, 512), BF16) for i in range(4)]
    vstage = [sb(f"vstage{i}", (128, 512), BF16) for i in range(4)]
    stats1 = sb("stats1", (128, 16), F32)
    assert cursor[0] <= SBUF_CAP, f"region A: {cursor[0]}"

    cursor[0] = DB
    ctx_sb = sb("ctx_sb", (128, NHC * S), BF16)
    K_sb = sb("K_sb", (128, S), BF16)
    V_sb = sb("V_sb", (128, S), BF16)
    qt_sb = [sb(f"qt_sb{i}", (128, 512), BF16) for i in range(2)]
    mrows_sb = sb("mrows_sb", (128, S), BF16)
    acol_sb = sb("acol_sb", (128, S // 128), F32)
    P_sb = sb("P_sb", (128, 3 * 512), BF16)
    r_bc = sb("r_bc", (128, 512), F32)
    fl = sb("fl", (128, 48), F32)
    cast_i8 = [sb(f"cast_i8{i}", (128, 2048), I8) for i in range(2)]
    cast_bf = [sb(f"cast_bf{i}", (128, 2048), BF16) for i in range(2)]
    masks_sb = sb("masks_sb", (128, 4 * 512), F32)
    wout_f32 = sb("wout_f32", (128, 2048), F32)
    wout_bf = sb("wout_bf", (128, 4 * 2048), BF16)
    outst3 = [sb(f"outst3_{i}", (128, 512), F32) for i in range(4)]
    assert cursor[0] <= SBUF_CAP, f"region B: {cursor[0]}"

    cursor[0] = DB
    ln2fm = sb("ln2fm", (128, 16 * Sq), BF16)
    fuse_sb = sb("fuse_sb", (128, 64 * 512), BF16)
    sd_bc = sb("sd_bc", (128, D), F32)
    bout_bc = sb("bout_bc", (128, D), F32)
    wgstage = [sb(f"wgstage{i}", (128, 16 * 128), BF16) for i in range(2)]
    wustage = [sb(f"wustage{i}", (128, 16 * 128), BF16) for i in range(2)]
    wd_sb = [sb(f"wd_sb{i}", (128, 2 * 512), BF16) for i in range(2)]
    rs_sb = sb("rs_sb", (128, D), F32)
    scr2 = rs_sb
    xq_sb = sb("xq_sb", (128, D), F32)
    rows_tmp = sb("rows_tmp", (1, D), F32, at=sb_off["xq_sb"])
    x1_sb = sb("x1_sb", (128, D), F32)
    ln2bf = sb("ln2bf", (128, D), BF16)
    stats2 = sb("stats2", (128, 16), F32)
    g_sb = [sb(f"g_sb{i}", (128, 512), BF16) for i in range(2)]
    x1t_sb = [sb(f"x1t{i}", (128, 512), F32) for i in range(2)]
    outst6 = [sb(f"outst6_{i}", (128, 512), F32) for i in range(2)]
    assert cursor[0] <= SBUF_CAP, f"region C: {cursor[0]}"

    ps_f = [nc.alloc_psum_tensor(f"ps_f{i}", [128, 512], F32) for i in range(6)]
    ps_b = [nc.alloc_psum_tensor("ps_b0", [128, 512], BF16),
            nc.alloc_psum_tensor("ps_b1", [128, 512], BF16)]

    E = Emitter()

    def ph0():
        loads = [
            (ident_sb[:, :], ident_in), (ones2_sb[:, :], ones2_in),
            (ones1_sb[:, :], ones1_in), (bqkvqk_sb[:, :], bqkvqk_in),
            (sg_sb[:, :], sg_in), (su_sb[:, :], su_in),
            (ones2p_sb[:, :], ones2p_in), (onesfull_sb[:, :], onesfull_in),
        ]
        for dst, src in loads:
            E.op("sync", lambda e, a=dst, b=src: e.dma_start(out=a, in_=b), inc="Lc", by=16)
        E.op("sync", lambda e: e.dma_start(
            out=wstage[0][0:1, 0:512], in_=bqkvv_row), inc="Lc", by=16)
        lc_total = E.cnt["Lc"]

        E.op("vector", lambda e: e.memset(eps_col[:, :], float(EPS)), inc="DVE")
        E.op("vector", lambda e: e.tensor_scalar_mul(bqs_sb[:, :], bqkvqk_sb[:, 0:4], float(SCALE)),
             waits=[("Lc", lc_total)], inc="DVE")
        pe1 = E.op("tensor", lambda e: e.matmul(ps_f[0][:, :], ones1_sb[:, :],
                                                wstage[0][0:1, 0:512], start=True, stop=True),
                   waits=[("Lc", lc_total)], inc="PE")
        bv_cp = E.op("vector", lambda e: e.tensor_copy(bv_bc[:, :], ps_f[0][:, :]),
                     waits=[("PE", pe1)], inc="DVE")

        pool_done = []
        for dc in range(16):
            slot = dc % 2
            w = [("POOL", pool_done[dc - 2])] if dc >= 2 else []
            if dc == 0:
                w = w + [("PE", pe1)]
            ld = E.op("sync", lambda e, s=slot, d=dc: e.dma_start(
                out=wstage[s][:, :], in_=wqkvT[d * 128:(d + 1) * 128, :]),
                waits=w, inc=f"Lw{slot}", by=16)
            pc = E.op("gpsimd", lambda e, s=slot, d=dc: e.tensor_copy(
                wqkv_sb[:, d * 1536:(d + 1) * 1536], wstage[s][:, :]),
                waits=[(f"Lw{slot}", ld)], inc="POOL")
            pool_done.append(pc)
        return E.cnt["POOL"], bv_cp

    wqkv_pool_cnt, bv_cp = ph0()

    def ph1():
        state = {"pe_last": 0}
        last_apply = {}
        qk_evac_cnt = {}
        v_evac_cnt = {}
        store_cnt = {}
        tp_evac = {}
        lnfm_user = {}
        xsb_user = {}
        xscr_user = {}

        def emit_ln(c):
            for ts in range(4):
                gidx = c * 4 + ts
                slot = gidx % 2
                par = gidx % 2
                ld = E.op("sync", lambda e, s=slot, t0=gidx * 128: e.dma_start(
                    out=x_sb[s][:, :], in_=x_full[t0:t0 + 128, :]),
                    waits=[("DVE", xsb_user.get(slot, 0))], inc=f"Lx{slot}", by=16)
                E.op("vector", lambda e, s=slot, p=par: e.tensor_reduce(
                    stats1[:, 0 + p:1 + p], x_sb[s][:, :], AX.X, ALU.add, negate=True),
                    waits=[(f"Lx{slot}", ld)], inc="DVE")
                nm_dve = E.op("vector", lambda e, p=par: e.tensor_scalar_mul(
                    stats1[:, 2 + p:3 + p], stats1[:, 0 + p:1 + p], 1.0 / D), inc="DVE")
                sq = E.op("scalar", lambda e, s=slot, p=par: e.activation(
                    xscr[s][:, :], x_sb[s][:, :], AF.Square,
                    bias=stats1[:, 2 + p:3 + p], scale=1.0, accum_out=stats1[:, 4 + p:5 + p]),
                    waits=[("DVE", nm_dve), ("ACT", xscr_user.get(slot, 0))], inc="ACT")
                E.op("scalar", lambda e, p=par: e.activation(
                    stats1[:, 6 + p:7 + p], stats1[:, 4 + p:5 + p], AF.Ln,
                    bias=eps_col[:, :], scale=1.0 / D), inc="ACT")
                rstd = E.op("scalar", lambda e, p=par: e.activation(
                    stats1[:, 8 + p:9 + p], stats1[:, 6 + p:7 + p], AF.Exp,
                    bias=0.0, scale=-0.5), inc="ACT")
                xscr_user[slot] = sq
                x5 = gidx % 5
                ap = E.op("vector", lambda e, s=slot, x=x5, p=par: e.tensor_scalar(
                    xln[x][:, :], x_sb[s][:, :], stats1[:, 2 + p:3 + p], stats1[:, 8 + p:9 + p],
                    ALU.add, ALU.mult),
                    waits=[("ACT", rstd), ("PE", state["pe_last"] if gidx >= 4 else 0)],
                    inc="DVE")
                last_apply[gidx] = ap
                xsb_user[slot] = ap

        def emit_qkv(c, ots):
            lbuf = c % 2
            for ot in ots:
                bank = ot % 4
                w = [("DVE", state[("evac", c)]), ("POOL", wqkv_pool_cnt),
                     ("ACT", qk_evac_cnt.get(bank, 0)), ("DVE", v_evac_cnt.get(bank, 0))]
                if ot >= 8:
                    w.append(("DVE", bv_cp))
                for dc in range(16):
                    if ot < 8:
                        lhs = wqkv_sb[:, dc * 1536 + ot * 128: dc * 1536 + (ot + 1) * 128]
                        rhs = lnfm[lbuf][:, dc * 512:(dc + 1) * 512]
                    else:
                        ts = ot - 8
                        lhs = lnfm[lbuf][:, dc * 512 + ts * 128: dc * 512 + (ts + 1) * 128]
                        rhs = wqkv_sb[:, dc * 1536 + 1024: dc * 1536 + 1536]
                    mm = E.op("tensor", lambda e, b=bank, l=lhs, r=rhs, d=dc: e.matmul(
                        ps_f[b][:, :], l, r, start=(d == 0), stop=(d == 15)),
                        waits=w, inc="PE")
                    w = []
                state["pe_last"] = mm
                lnfm_user[lbuf] = mm
                if ot < 4:
                    ev = E.op("scalar", lambda e, b=bank, o=ot: e.activation(
                        qkstage[b][:, :], ps_f[b][:, :], AF.Identity,
                        bias=bqs_sb[:, o:o + 1], scale=float(SCALE)),
                        waits=[("PE", mm), (f"Gq{bank}", store_cnt.get(("qk", bank), 0))],
                        inc="ACT")
                    qk_evac_cnt[bank] = ev
                    store_cnt[("qk", bank)] = E.op("gpsimd", lambda e, b=bank, o=ot, cc=c: e.dma_start(
                        out=qfm.ap()[o * 128:(o + 1) * 128, cc * 512:(cc + 1) * 512],
                        in_=qkstage[b][:, :]), waits=[("ACT", ev)], inc=f"Gq{bank}", by=16)
                elif ot < 8:
                    ev = E.op("scalar", lambda e, b=bank, o=ot: e.activation(
                        qkstage[b][:, :], ps_f[b][:, :], AF.Identity,
                        bias=bqkvqk_sb[:, o:o + 1], scale=1.0),
                        waits=[("PE", mm), (f"Gq{bank}", store_cnt.get(("qk", bank), 0))],
                        inc="ACT")
                    qk_evac_cnt[bank] = ev
                    store_cnt[("qk", bank)] = E.op("gpsimd", lambda e, b=bank, o=ot, cc=c: e.dma_start(
                        out=kfm.ap()[(o - 4) * 128:(o - 3) * 128, cc * 512:(cc + 1) * 512],
                        in_=qkstage[b][:, :]), waits=[("ACT", ev)], inc=f"Gq{bank}", by=16)
                else:
                    ts = ot - 8
                    ev = E.op("vector", lambda e, b=bank, t=ts: e.tensor_tensor(
                        vstage[t][:, :], ps_f[b][:, :], bv_bc[:, :], ALU.add),
                        waits=[("PE", mm), (f"Gv{ts}", store_cnt.get(("v", ts), 0))], inc="DVE")
                    v_evac_cnt[bank] = ev
                    store_cnt[("v", ts)] = E.op("gpsimd", lambda e, t=ts, cc=c: e.dma_start(
                        out=vtm.ap()[cc * 512 + t * 128: cc * 512 + (t + 1) * 128, :],
                        in_=vstage[t][:, :]), waits=[("DVE", ev)], inc=f"Gv{ts}", by=16)

        def emit_transposes(c):
            lbuf = c % 2
            all_evac = 0
            for dc in range(16):
                bk = dc % 2
                w = [("DVE", last_apply[c * 4 + 3]), ("DVE", tp_evac.get(bk, 0))]
                for ts in range(4):
                    x5 = (c * 4 + ts) % 5
                    last = E.op("tensor", lambda e, k=bk, t=ts, x=x5, d=dc: e.transpose(
                        ps_b[k][:, t * 128:(t + 1) * 128],
                        xln[x][:, d * 128:(d + 1) * 128], ident_sb[:, :]),
                        waits=w, inc="PE")
                    w = []
                ev = E.op("vector", lambda e, k=bk, d=dc, b=lbuf: e.tensor_copy(
                    lnfm[b][:, d * 512:(d + 1) * 512], ps_b[k][:, :]),
                    waits=[("PE", last), ("PE", lnfm_user.get(lbuf, 0))], inc="DVE")
                tp_evac[bk] = ev
                all_evac = ev
            state[("evac", c)] = all_evac

        emit_ln(0)
        for c in range(NC):
            emit_transposes(c)
            emit_qkv(c, range(0, 8))
            if c + 1 < NC:
                emit_ln(c + 1)
            emit_qkv(c, range(8, 12))
        waits = [(f"Gq{b}", E.cnt.get(f"Gq{b}", 0)) for b in range(4)]
        waits += [(f"Gv{t}", E.cnt.get(f"Gv{t}", 0)) for t in range(4)]
        return waits

    ph1_store_waits = ph1()

    cast_jobs = [("wout", ic) for ic in range(4)]
    cast_jobs += [("wg", rc, cc) for rc in range(16) for cc in range(4)]
    cast_jobs += [("wu", rc, cc) for rc in range(16) for cc in range(4)]
    cast_jobs += [("wd", rc) for rc in range(64)]

    cast_state = {"i": 0, "pool_prev": {}, "store_prev": {}, "wout_f32_user": 0,
                  "wout_pool": 0}

    def emit_cast_jobs(n):
        for _ in range(n):
            i = cast_state["i"]
            if i >= len(cast_jobs):
                return
            cast_state["i"] = i + 1
            job = cast_jobs[i]
            if job[0] == "wout":
                ic = job[1]
                ld = E.op("sync", lambda e, c=ic: e.dma_start(
                    out=wout_f32[:, :], in_=woutT[c * 128:(c + 1) * 128, :]),
                    waits=[("POOL", cast_state["wout_f32_user"])], inc="Lo", by=16)
                pc = E.op("gpsimd", lambda e, c=ic: e.tensor_copy(
                    wout_bf[:, c * 2048:(c + 1) * 2048], wout_f32[:, :]),
                    waits=[("Lo", ld)], inc="POOL")
                cast_state["wout_f32_user"] = pc
                cast_state["wout_pool"] = pc
                continue
            slot = i % 2
            if job[0] == "wd":
                rc = job[1]
                src = wd8[rc * 128:(rc + 1) * 128, :]
                dst = wd_bf.ap()[rc * 128:(rc + 1) * 128, :]
            else:
                rc, cc = job[1], job[2]
                g8 = wg8 if job[0] == "wg" else wu8
                gb = wg_bf if job[0] == "wg" else wu_bf
                src = g8[rc * 128:(rc + 1) * 128, cc * 2048:(cc + 1) * 2048]
                dst = gb.ap()[rc * 128:(rc + 1) * 128, cc * 2048:(cc + 1) * 2048]
            ld = E.op("sync", lambda e, s=slot, a=src: e.dma_start(
                out=cast_i8[s][:, :], in_=a),
                waits=[("POOL", cast_state["pool_prev"].get(slot, 0))], inc=f"Li{slot}", by=16)
            pc = E.op("gpsimd", lambda e, s=slot: e.tensor_copy(
                cast_bf[s][:, :], cast_i8[s][:, :]),
                waits=[(f"Li{slot}", ld), (f"Gc{slot}", cast_state["store_prev"].get(slot, 0))],
                inc="POOL")
            cast_state["pool_prev"][slot] = pc
            cast_state["store_prev"][slot] = E.op("gpsimd", lambda e, s=slot, d=dst: e.dma_start(
                out=d, in_=cast_bf[s][:, :]), waits=[("POOL", pc)], inc=f"Gc{slot}", by=16)

    def ph2():
        st = {"exp": {}, "qt_user": {}, "ctx_evac": 0, "ctx_free": 0, "l_free": 0,
              "t": 0, "pv_last": 0, "bc_free": 0}
        cast_per_head = (len(cast_jobs) + NHC - 1) // NHC

        def stage_x(info):
            kg, diag_kt, qt = info["kg"], info["diag_kt"], info["qt"]
            bank = st["t"] % 3
            st["t"] += 1
            info["bank"] = bank
            q0 = qt * 512
            w = [("ACT", st["exp"].get(bank, 0)), (f"Lq{info['slot']}", info["qld"]),
                 ("Lk", info["kld"]), ("La", info["ald"]), ("Lm", info["mld"]),
                 ("DVE", st["mz"])]
            E.op("tensor", lambda e, b=bank, k=kg, s=info["slot"]: e.matmul(
                ps_f[b][:, :], K_sb[:, k * 128:(k + 1) * 128], qt_sb[s][:, :],
                start=True, stop=False), waits=w, inc="PE")
            qk = E.op("tensor", lambda e, b=bank, q=q0: e.matmul(
                ps_f[b][:, :], ones2p_sb[:, :], mrows_sb[:, q:q + 512],
                start=False, stop=True), inc="PE")
            mask_c = 0
            if diag_kt is not None:
                mask_c = E.op("vector", lambda e, b=bank, m=diag_kt: e.tensor_tensor(
                    ps_f[b][:, :], ps_f[b][:, :], masks_sb[:, m * 512:(m + 1) * 512],
                    ALU.add), waits=[("PE", qk), ("Lc", st["mkld"])], inc="DVE")
            slot3 = st["t"] % 3
            info["p3"] = slot3
            st["exp"][bank] = E.op("scalar", lambda e, b=bank, k=kg, p3=slot3: e.activation(
                P_sb[:, p3 * 512:(p3 + 1) * 512], ps_f[b][:, :], AF.Exp,
                bias=acol_sb[:, k:k + 1], scale=1.0),
                waits=[("PE", qk), ("DVE", mask_c), ("PE", st["pv_last"])], inc="ACT")
            info["exp_c"] = st["exp"][bank]

        def stage_y(info):
            kg, first, last = info["kg"], info["first"], info["last"]
            w = [("ACT", info["exp_c"]), ("Lv", info["vld"])]
            if first:
                w += [("DVE", st["ctx_free"])]
            pv = E.op("tensor", lambda e, k=kg, p3=info["p3"], st2=first, sp2=last: e.matmul(
                ps_f[3][:, :], V_sb[:, k * 128:(k + 1) * 128],
                P_sb[:, p3 * 512:(p3 + 1) * 512],
                start=st2, stop=sp2), waits=w, inc="PE")
            w = [("DVE", st["l_free"])] if first else []
            st["pv_last"] = E.op("tensor", lambda e, p3=info["p3"], st2=first, sp2=last: e.matmul(
                ps_f[4][:, :], onesfull_sb[:, :], P_sb[:, p3 * 512:(p3 + 1) * 512],
                start=st2, stop=sp2), waits=w, inc="PE")

        def epilogue(info):
            h, qt = info["h"], info["qt"]
            q0 = qt * 512
            pe_pv = E.cnt.get("PE", 0)
            rc = E.op("vector", lambda e: e.reciprocal(r_bc[:, :], ps_f[4][:, :]),
                      waits=[("PE", pe_pv)], inc="DVE")
            st["l_free"] = rc
            st["ctx_evac"] = E.op("vector", lambda e, hh=h, qq=q0: e.tensor_tensor(
                ctx_sb[:, hh * S + qq: hh * S + qq + 512], ps_f[3][:, :], r_bc[:, :],
                ALU.mult), waits=[("DVE", rc)], inc="DVE")
            st["ctx_free"] = st["ctx_evac"]

        pend = []
        mkld = E.op("sync", lambda e: e.dma_start(
            out=masks_sb[:, :].rearrange("p (q k) -> p q k", q=4),
            in_=masks_in.rearrange("q p k -> p q k")),
            waits=ph1_store_waits, inc="Lc", by=16)
        st["mkld"] = mkld
        mz = E.op("vector", lambda e: e.memset(mrows_sb[:, :], 0.0), inc="DVE")
        st["mz"] = mz
        def drain_all():
            while pend:
                info = pend.pop(0)
                stage_y(info)
                if info["last"]:
                    epilogue(info)

        for h in range(NHC):
            drain_all()
            prev_pe = E.cnt.get("PE", 0)
            kld = E.op("sync", lambda e, hh=h: e.dma_start(
                out=K_sb[:, :], in_=kfm.ap()[hh * 128:(hh + 1) * 128, :]),
                waits=ph1_store_waits + [("PE", prev_pe)], inc="Lk", by=16)
            vld = E.op("sync", lambda e, hh=h: e.dma_start(
                out=V_sb[:, :].rearrange("p (ks d) -> p ks d", ks=NKS),
                in_=vtm.ap()[:, hh * 128:(hh + 1) * 128].rearrange("(ks p) d -> p ks d", p=128)),
                inc="Lv", by=16)
            ald = E.op("sync", lambda e, hh=h: e.dma_start(
                out=acol_sb[:, :], in_=acol_in[hh]), inc="La", by=16)
            mld = E.op("sync", lambda e, hh=h: e.dma_start(
                out=mrows_sb[0:2, :], in_=mrows_in[hh]),
                waits=[("DVE", st["mz"])], inc="Lm", by=16)
            n_cast_emitted = 0
            for qt in range(NQTILE):
                slot = qt % 2
                qld = E.op("sync", lambda e, hh=h, qq=qt * 512, s=slot: e.dma_start(
                    out=qt_sb[s][:, :], in_=qfm.ap()[hh * 128:(hh + 1) * 128, qq:qq + 512]),
                    waits=[("PE", st["qt_user"].get(slot, 0))], inc=f"Lq{slot}", by=16)
                want = ((qt + 1) * cast_per_head) // NQTILE
                emit_cast_jobs(want - n_cast_emitted)
                n_cast_emitted = want
                nkt = (qt + 1) * 4
                for t in range(nkt):
                    c, kt = t // 4, t % 4
                    diag_kt = kt if c == qt else None
                    info = {"kg": t, "qt": qt, "slot": slot, "kld": kld, "ald": ald,
                            "qld": qld, "vld": vld, "mld": mld, "h": h,
                            "diag_kt": diag_kt, "first": (t == 0), "last": (t == nkt - 1)}
                    stage_x(info)
                    if len(pend) >= 2:
                        old = pend.pop(0)
                        stage_y(old)
                        if old["last"]:
                            epilogue(old)
                    pend.append(info)
                st["qt_user"][slot] = E.cnt.get("PE", 0)
        drain_all()
        emit_cast_jobs(len(cast_jobs))
        return st["ctx_evac"]

    ctx_done = ph2()

    def ph3():
        bank_user = {}
        st_user = {}
        half_waits = []
        NRS = 4
        for half in range(NRS):
            for gq in range(4):
                for tr in range(NQT // NRS):
                    tt = gq * NQT + half * (NQT // NRS) + tr
                    row0 = half * (S // NRS) + gq * (S // (4 * NRS)) + tr * 128
                    for oc in range(4):
                        bank = (tt * 4 + oc) % 4
                        w = [("DVE", ctx_done), ("POOL", cast_state["wout_pool"]),
                             ("ACT", bank_user.get(bank, 0))]
                        for ic in range(4):
                            mm = E.op("tensor", lambda e, b=bank, i=ic, t=tt, o=oc: e.matmul(
                                ps_f[b][:, :], ctx_sb[:, i * S + t * 128:i * S + t * 128 + 128],
                                wout_bf[:, i * 2048 + o * 512:i * 2048 + (o + 1) * 512],
                                start=(i == 0), stop=(i == 3)), waits=w, inc="PE")
                            w = []
                        slot = (tt * 4 + oc) % 4
                        sem = f"S3{half}_{slot}"
                        prev = st_user.get(slot, ("S30_0", 0))
                        ev = E.op("scalar", lambda e, b=bank, sl=slot: e.activation(
                            outst3[sl][:, :], ps_f[b][:, :], AF.Copy, bias=0.0, scale=1.0),
                            waits=[("PE", mm), prev], inc="ACT")
                        bank_user[bank] = ev
                        stc = E.op("gpsimd", lambda e, sl=slot, r0=row0, o=oc: e.dma_start(
                            out=rs_in.ap()[r0:r0 + 128, o * 512:(o + 1) * 512],
                            in_=outst3[sl][:, :]), waits=[("ACT", ev)], inc=sem, by=16)
                        st_user[slot] = (sem, stc)
            hw = [(f"S3{half}_{sl}", E.cnt.get(f"S3{half}_{sl}", 0)) for sl in range(4)]
            half_waits.append(hw)
        cc = 0
        for half in range(NRS):
            r0, r1 = half * (S // NRS), (half + 1) * (S // NRS)
            o0, o1 = half * (Sq // NRS), (half + 1) * (Sq // NRS)
            cc = E.op("gpsimd", lambda e, a=r0, b=r1, c2=o0, d=o1: e.collective_compute(
                "ReduceScatter", ALU.add,
                replica_groups=[[0, 1, 2, 3], [4, 5, 6, 7]],
                ins=[rs_in.ap()[a:b, :].opt()], outs=[rs_out.ap()[c2:d, :].opt()]),
                waits=half_waits[half], inc="CC")
        return cc

    rs_done = ph3()
    ph3_pe_end = E.cnt.get("PE", 0)

    def ph46():
        ph3_act = E.cnt.get("ACT", 0)
        cast_done = [("Gc0", E.cnt.get("Gc0", 0)), ("Gc1", E.cnt.get("Gc1", 0)),
                     ("POOL", E.cnt.get("POOL", 0))]
        cast_waits = [("Gc0", E.cnt.get("Gc0", 0)), ("Gc1", E.cnt.get("Gc1", 0))]
        ph3_stores = [(f"S3{i}_{j}", E.cnt.get(f"S3{i}_{j}", 0))
                      for i in range(4) for j in range(4)]
        ph3_pe = [("PE", ph3_pe_end)]
        ld = E.op("sync", lambda e: e.dma_start(out=rows_tmp[0:1, :], in_=sd_row_in),
                  waits=cast_done + [("DVE", ctx_done)] + ph3_stores, inc="Lr", by=16)
        bank_cp = {}
        for oc in range(4):
            bank = oc % 2
            pe = E.op("tensor", lambda e, o=oc, b=bank: e.matmul(
                ps_f[b][:, :], ones1_sb[:, :], rows_tmp[0:1, o * 512:(o + 1) * 512],
                start=True, stop=True),
                waits=[("Lr", ld), ("ACT", ph3_act), ("DVE", bank_cp.get(bank, 0))], inc="PE")
            bank_cp[bank] = E.op("vector", lambda e, o=oc, b=bank: e.tensor_copy(
                sd_bc[:, o * 512:(o + 1) * 512], ps_f[b][:, :]), waits=[("PE", pe)], inc="DVE")
        sd_pe = E.cnt.get("PE", 0)
        ld2 = E.op("sync", lambda e: e.dma_start(out=rows_tmp[0:1, :], in_=bout_row_in),
                   waits=[("PE", sd_pe)], inc="Lr", by=16)
        for oc in range(4):
            bank = oc % 2
            pe = E.op("tensor", lambda e, o=oc, b=bank: e.matmul(
                ps_f[b][:, :], ones1_sb[:, :], rows_tmp[0:1, o * 512:(o + 1) * 512],
                start=True, stop=True),
                waits=[("Lr", ld2), ("DVE", bank_cp.get(bank, 0))], inc="PE")
            bank_cp[bank] = E.op("vector", lambda e, o=oc, b=bank: e.tensor_copy(
                bout_bc[:, o * 512:(o + 1) * 512], ps_f[b][:, :]), waits=[("PE", pe)], inc="DVE")
        bcast_done = max(bank_cp.values())
        rows_pe = E.cnt.get("PE", 0)

        p4 = {"a1": 0, "a2": 0, "x1st": 0, "tp": 0, "sq": 0, "tp_ev": {}}
        ln2_tile = {}
        x1st_tile = {}

        def emit_ph4_tile(tt):
            par = tt % 2
            rld = E.op("scalar", lambda e, t=tt: e.dma_start(
                out=rs_sb[:, :], in_=rs_out.ap()[t * 128:(t + 1) * 128, :]),
                waits=[("CC", 1 + tt * 4 // NQT), ("DVE", p4["a1"]),
                       ("ACT", p4["sq"])], inc="Ls", by=16)
            xld = E.op("scalar", lambda e, t=tt: e.dma_start(
                out=xq_sb[:, :], in_=x_q[t * 128:(t + 1) * 128, :]),
                waits=[("DVE", p4["a2"]), ("PE", rows_pe)], inc="Lxq", by=16)
            a1 = E.op("vector", lambda e: e.tensor_tensor(
                x1_sb[:, :], rs_sb[:, :], bout_bc[:, :], ALU.add),
                waits=[("Ls", rld), ("Sx1", p4["x1st"]), ("DVE", bcast_done)], inc="DVE")
            p4["a1"] = a1
            a2 = E.op("vector", lambda e: e.tensor_tensor(
                x1_sb[:, :], x1_sb[:, :], xq_sb[:, :], ALU.add),
                waits=[("Lxq", xld)], inc="DVE")
            p4["a2"] = a2
            p4["x1st"] = E.op("scalar", lambda e, t=tt: e.dma_start(
                out=x1d.ap()[t * 128:(t + 1) * 128, :], in_=x1_sb[:, :]),
                waits=[("DVE", a2)], inc="Sx1", by=16)
            x1st_tile[tt] = p4["x1st"]
            E.op("vector", lambda e, p=par: e.tensor_reduce(
                stats2[:, 0 + p:1 + p], x1_sb[:, :], AX.X, ALU.add, negate=True), inc="DVE")
            nm = E.op("vector", lambda e, p=par: e.tensor_scalar_mul(
                stats2[:, 2 + p:3 + p], stats2[:, 0 + p:1 + p], 1.0 / D), inc="DVE")
            sq = E.op("scalar", lambda e, p=par: e.activation(
                scr2[:, :], x1_sb[:, :], AF.Square,
                bias=stats2[:, 2 + p:3 + p], scale=1.0, accum_out=stats2[:, 4 + p:5 + p]),
                waits=[("DVE", nm)], inc="ACT")
            p4["sq"] = sq
            E.op("scalar", lambda e, p=par: e.activation(
                stats2[:, 6 + p:7 + p], stats2[:, 4 + p:5 + p], AF.Ln,
                bias=eps_col[:, :], scale=1.0 / D), inc="ACT")
            rstd = E.op("scalar", lambda e, p=par: e.activation(
                stats2[:, 8 + p:9 + p], stats2[:, 6 + p:7 + p], AF.Exp,
                bias=0.0, scale=-0.5), inc="ACT")
            ap = E.op("vector", lambda e, p=par: e.tensor_scalar(
                ln2bf[:, :], x1_sb[:, :], stats2[:, 2 + p:3 + p], stats2[:, 8 + p:9 + p],
                ALU.add, ALU.mult), waits=[("ACT", rstd), ("PE", p4["tp"])], inc="DVE")
            for dg in range(4):
                bk = dg % 2
                w = [("DVE", ap), ("DVE", p4["tp_ev"].get(bk, 0))]
                for j in range(4):
                    dc = dg * 4 + j
                    tl = E.op("tensor", lambda e, k2=bk, jj=j, d=dc: e.transpose(
                        ps_b[k2][:, jj * 128:(jj + 1) * 128],
                        ln2bf[:, d * 128:(d + 1) * 128], ident_sb[:, :]),
                        waits=w, inc="PE")
                    w = []
                ev = E.op("vector", lambda e, k2=bk, g=dg, t=tt: e.tensor_copy(
                    ln2fm[:, :].rearrange("p (dc t) -> p dc t", dc=16)[
                        :, g * 4:(g + 1) * 4, t * 128:(t + 1) * 128],
                    ps_b[k2][:, :].rearrange("p (a t) -> p a t", a=4)),
                    waits=[("PE", tl)], inc="DVE")
                p4["tp_ev"][bk] = ev
            ln2_tile[tt] = max(p4["tp_ev"].get(0, 0), p4["tp_ev"].get(1, 0))
            p4["tp"] = E.cnt.get("PE", 0)

        p6 = {"wg_user": {}, "bank_act": {}, "bank_dve": {}, "fuse_pe": 0, "x1_user": {}}

        def emit_tg(tg):
            t0 = tg * 512
            gate_gate = [("DVE", ln2_tile[(tg + 1) * 4 - 1]), ("DVE", bcast_done)]
            for jc in range(64):
                slot = jc % 2
                gb = jc % 2
                ub = 2 + jc % 2
                wld = E.op("sync", lambda e, j=jc, s=slot: e.dma_start(
                    out=wgstage[s][:, :].rearrange("p (a j) -> p a j", a=16),
                    in_=wg_bf.ap()[:, j * 128:(j + 1) * 128].rearrange("(a p) j -> p a j", p=128)),
                    waits=cast_waits + ph3_pe + [("PE", p6["wg_user"].get(("g", slot), 0))],
                    inc=f"Lg{slot}", by=16)
                uld = E.op("sync", lambda e, j=jc, s=slot: e.dma_start(
                    out=wustage[s][:, :].rearrange("p (a j) -> p a j", a=16),
                    in_=wu_bf.ap()[:, j * 128:(j + 1) * 128].rearrange("(a p) j -> p a j", p=128)),
                    waits=[("PE", p6["wg_user"].get(("u", slot), 0))], inc=f"Lu{slot}", by=16)
                w = gate_gate + [(f"Lg{slot}", wld), ("ACT", p6["bank_act"].get(gb, 0))]
                for dc in range(16):
                    mmg = E.op("tensor", lambda e, b=gb, d=dc, s=slot, t=t0: e.matmul(
                        ps_f[b][:, :], wgstage[s][:, d * 128:(d + 1) * 128],
                        ln2fm[:, d * Sq + t:d * Sq + t + 512],
                        start=(d == 0), stop=(d == 15)), waits=w, inc="PE")
                    w = []
                p6["wg_user"][("g", slot)] = mmg
                w = [(f"Lu{slot}", uld), ("DVE", p6["bank_dve"].get(ub, 0))]
                for dc in range(16):
                    mmu = E.op("tensor", lambda e, b=ub, d=dc, s=slot, t=t0: e.matmul(
                        ps_f[b][:, :], wustage[s][:, d * 128:(d + 1) * 128],
                        ln2fm[:, d * Sq + t:d * Sq + t + 512],
                        start=(d == 0), stop=(d == 15)), waits=w, inc="PE")
                    w = []
                p6["wg_user"][("u", slot)] = mmu
                gslot = jc % 2
                sl = E.op("scalar", lambda e, b=gb, g=gslot, j=jc: e.activation(
                    g_sb[g][:, :], ps_f[b][:, :], AF.Silu,
                    bias=0.0, scale=sg_sb[:, j:j + 1]),
                    waits=[("PE", mmg), ("DVE", p6["bank_act"].get(("gs", gslot), 0))],
                    inc="ACT")
                p6["bank_act"][gb] = sl
                fu = E.op("vector", lambda e, b=ub, g=gslot, j=jc: e.scalar_tensor_tensor(
                    fuse_sb[:, j * 512:(j + 1) * 512], ps_f[b][:, :], su_sb[:, j:j + 1],
                    g_sb[g][:, :], ALU.mult, ALU.mult),
                    waits=[("PE", mmu), ("ACT", sl), ("PE", p6["fuse_pe"])], inc="DVE")
                p6["bank_dve"][ub] = fu
                p6["bank_act"][("gs", gslot)] = fu
            fuse_done = E.cnt.get("DVE", 0)
            if tg + 1 < NTG:
                for tt in range((tg + 1) * 4, (tg + 2) * 4):
                    emit_ph4_tile(tt)
            ev_dve = {}
            st_user = {}
            x1gate = [("Sx1", x1st_tile[(tg + 1) * 4 - 1])]
            for oc in range(4):
                mm = 0
                for jsb in range(32):
                    slot = jsb % 2
                    wld = E.op("sync", lambda e, j=jsb, o=oc, s=slot: e.dma_start(
                        out=wd_sb[s][:, :].rearrange("p (a o) -> p a o", a=2),
                        in_=wd_bf.ap()[j * 256:(j + 1) * 256, o * 512:(o + 1) * 512]
                        .rearrange("(a p) o -> p a o", p=128)),
                        waits=cast_waits + [("PE", p6["wg_user"].get(("wd", slot), 0))],
                        inc=f"Ld{slot}", by=16)
                    for js in range(2):
                        jg = jsb * 2 + js
                        for ts in range(4):
                            w = []
                            if jg == 0:
                                w = [("DVE", fuse_done), (f"Ld{slot}", wld),
                                     ("DVE", ev_dve.get(ts, 0))]
                            elif js == 0 and ts == 0:
                                w = [(f"Ld{slot}", wld)]
                            mm = E.op("tensor", lambda e, j=jg, s=slot, jj=js, t=ts: e.matmul(
                                ps_f[t][:, :],
                                fuse_sb[:, j * 512 + t * 128:j * 512 + (t + 1) * 128],
                                wd_sb[s][:, jj * 512:(jj + 1) * 512],
                                start=(j == 0), stop=(j == 63)), waits=w, inc="PE")
                    p6["wg_user"][("wd", slot)] = mm
                down_pe = mm
                for ts in range(4):
                    slot = ts % 2
                    xld = E.op("sync", lambda e, t=ts, o=oc, g=tg, s=slot: e.dma_start(
                        out=x1t_sb[s][:, :],
                        in_=x1d.ap()[g * 512 + t * 128:g * 512 + (t + 1) * 128,
                                     o * 512:(o + 1) * 512]),
                        waits=x1gate + [("DVE", p6["x1_user"].get(slot, 0))],
                        inc=f"Lt{slot}", by=16)
                    d1 = E.op("vector", lambda e, t=ts, o=oc, s=slot: e.tensor_tensor(
                        outst6[s][:, :], ps_f[t][:, :], sd_bc[:, o * 512:(o + 1) * 512],
                        ALU.mult), waits=[("PE", down_pe), (f"S6{slot}", st_user.get(slot, 0))],
                        inc="DVE")
                    ev_dve[ts] = d1
                    d2 = E.op("vector", lambda e, s=slot, t=ts: e.tensor_tensor(
                        outst6[s][:, :], outst6[s][:, :], x1t_sb[s][:, :], ALU.add),
                        waits=[(f"Lt{slot}", xld)], inc="DVE")
                    p6["x1_user"][slot] = d2
                    st_user[slot] = E.op("scalar", lambda e, t=ts, o=oc, g=tg, s=slot: e.dma_start(
                        out=out[g * 512 + t * 128:g * 512 + (t + 1) * 128, o * 512:(o + 1) * 512],
                        in_=outst6[s][:, :]), waits=[("DVE", d2)], inc=f"S6{slot}", by=16)
            p6["fuse_pe"] = down_pe

        for tt in range(4):
            emit_ph4_tile(tt)
        for tg in range(NTG):
            emit_tg(tg)

    ph46()

    final_keys = ["Sx1", "S60", "S61"] + [f"S3{i}_{j}" for i in range(4) for j in range(4)]
    final_waits = [(k, E.cnt[k]) for k in final_keys if k in E.cnt]

    sem_keys = set(E.cnt.keys())
    for e in ENGINES:
        for _, ws, _, _ in E.ops[e]:
            for s, _v in ws:
                sem_keys.add(s)
    sem_keys = sorted(sem_keys)

    with ExitStack() as stk:
        sems = {s: stk.enter_context(nc.semaphore(f"sem_{s}")) for s in sem_keys}
        block = stk.enter_context(nc.Block())

        def run_engine(handle, eng):
            for fn, waits, inc, by in E.ops[eng]:
                for s, v in waits:
                    handle.wait_ge(sems[s], v)
                ins = fn(handle)
                if inc is not None:
                    ins.then_inc(sems[inc], by)
            if eng in ("scalar", "gpsimd"):
                for s, v in final_waits:
                    handle.wait_ge(sems[s], v)

        block.sync(lambda e: run_engine(e, "sync"))
        block.tensor(lambda e: run_engine(e, "tensor"))
        block.vector(lambda e: run_engine(e, "vector"))
        block.scalar(lambda e: run_engine(e, "scalar"))
        block.gpsimd(lambda e: run_engine(e, "gpsimd"))

    return nc



MBOUND_C = 45.0


def _acol(S, g):
    pos = np.arange(1 - S, 1, dtype=np.float32)
    outm = np.zeros((NHC, 128, S // 128), dtype=np.float32)
    for j in range(NHC):
        hglob = 4 * g + j
        slope = np.float32(1.0 / 2.0 ** np.float32((hglob + 1) * 0.5))
        outm[j] = (pos * slope).reshape(S // 128, 128).T
    return outm


def _mrows(S, g):
    pos = np.arange(1 - S, 1, dtype=np.float32)
    outa = np.zeros((NHC, 2, S), dtype=BFNP)
    for j in range(NHC):
        hglob = 4 * g + j
        slope = np.float32(1.0 / 2.0 ** np.float32((hglob + 1) * 0.5))
        m = -(pos * slope + np.float32(MBOUND_C))
        hi = m.astype(BFNP)
        lo = (m - hi.astype(np.float32)).astype(BFNP)
        outa[j, 0] = hi
        outa[j, 1] = lo
    return outa


def _masks():
    m = np.empty((4, 128, 512), dtype=np.float32)
    i = np.arange(128)[:, None]
    j = np.arange(512)[None, :]
    for kt in range(4):
        m[kt] = np.where(i <= j - kt * 128, 0.0, MASKVAL)
    return m


def kernel(L0_attn_x, L0_attn_Wqkv, L0_attn_b_qkv, L0_attn_Wout, L0_attn_b_out,
           L0_ffn_w_gate, L0_ffn_w_up, L0_ffn_w_down,
           L0_ffn_s_gate, L0_ffn_s_up, L0_ffn_s_down):
    x = np.ascontiguousarray(np.asarray(L0_attn_x, dtype=np.float32))
    B, S, _ = x.shape
    Sq = S // 4
    Wqkv = np.asarray(L0_attn_Wqkv, dtype=np.float32)
    b_qkv = np.asarray(L0_attn_b_qkv, dtype=np.float32)
    Wout = np.asarray(L0_attn_Wout, dtype=np.float32)
    b_out = np.asarray(L0_attn_b_out, dtype=np.float32)
    wg = np.ascontiguousarray(np.asarray(L0_ffn_w_gate))
    wu = np.ascontiguousarray(np.asarray(L0_ffn_w_up))
    wd = np.ascontiguousarray(np.asarray(L0_ffn_w_down))
    sg = np.asarray(L0_ffn_s_gate, dtype=np.float32)
    su = np.asarray(L0_ffn_s_up, dtype=np.float32)
    sd = np.asarray(L0_ffn_s_down, dtype=np.float32)

    nc = build_program(S)

    masks = _masks()
    ident = np.eye(128, dtype=BFNP)
    ones2 = np.ones((2, 128), dtype=BFNP)
    ones1 = np.ones((1, 128), dtype=np.float32)
    sg_cols = np.ascontiguousarray(sg.reshape(64, 128).T)
    su_cols = np.ascontiguousarray(su.reshape(64, 128).T)
    sd_row = np.ascontiguousarray(sd.reshape(1, D))
    bout_row = np.ascontiguousarray(b_out.reshape(1, D))

    in_maps = []
    for c in range(8):
        b, g = c // 4, c % 4
        q_rows = Wqkv[512 * g:512 * (g + 1)]
        k_rows = Wqkv[D + 512 * g:D + 512 * (g + 1)]
        v_rows = Wqkv[2 * D + 512 * g:2 * D + 512 * (g + 1)]
        wqkvT_s = np.ascontiguousarray(np.concatenate([q_rows, k_rows, v_rows], axis=0).T)
        bq = b_qkv[512 * g:512 * (g + 1)]
        bk = b_qkv[D + 512 * g:D + 512 * (g + 1)]
        bqkvqk = np.ascontiguousarray(
            np.concatenate([bq.reshape(4, 128), bk.reshape(4, 128)], axis=0).T)
        bqkvv_row = np.ascontiguousarray(
            b_qkv[2 * D + 512 * g:2 * D + 512 * (g + 1)].reshape(1, 512))
        woutT_s = np.ascontiguousarray(Wout[:, 512 * g:512 * (g + 1)].T)
        in_maps.append(dict(
            x_full=x[b], x_q=np.ascontiguousarray(x[b, g * Sq:(g + 1) * Sq]),
            wqkvT=wqkvT_s, bqkvqk=bqkvqk, bqkvv_row=bqkvv_row,
            woutT=woutT_s, bout_row=bout_row,
            acol=_acol(S, g), mrows=_mrows(S, g), masks=masks, ident=ident,
            ones2=ones2, ones1=ones1,
            ones2p=np.concatenate([np.ones((2, 128), dtype=BFNP),
                                   np.zeros((126, 128), dtype=BFNP)]),
            onesfull=np.ones((128, 128), dtype=BFNP),
            wg8=wg, wu8=wu, wd8=wd,
            sg_cols=sg_cols, su_cols=su_cols, sd_row=sd_row,
        ))

    res = run_bass_kernel_spmd(nc, in_maps, core_ids=list(range(8)))

    y = np.empty((B, S, D), dtype=np.float32)
    for c in range(8):
        b, g = c // 4, c % 4
        y[b, g * Sq:(g + 1) * Sq] = res.results[c]["out"]
    return y
